# version 22
# baseline (speedup 1.0000x reference)
import numpy as np
import ml_dtypes
from contextlib import ExitStack

import concourse.bass as bass
import concourse.mybir as mybir
from concourse.bass_utils import run_bass_kernel_spmd

F32 = mybir.dt.float32
BF16 = mybir.dt.bfloat16
AF = mybir.ActivationFunctionType
ALU = mybir.AluOpType
AX = mybir.AxisListType

D = 1024
FF = 4096
NJ = 8
NFT = 32
ENGS = ("sync", "gpsimd", "vector", "scalar", "tensor")


class Sem:
    __slots__ = ("name", "h", "n")

    def __init__(self, name):
        self.name = name
        self.h = None
        self.n = 0


class Emitter:
    def __init__(self):
        self.ops = {e: [] for e in ENGS}
        self.sems = []
        self.esem = {}
        self.waited = {e: {} for e in ENGS}
        self.W = {}
        self.R = {}

    def sem(self, name):
        s = Sem(name)
        self.sems.append(s)
        return s

    def _waits(self, engine, reads, writes, extra):
        acc = {}

        def add(sv):
            if sv is None:
                return
            s, v = sv
            if v <= 0:
                return
            cur = acc.get(s.name)
            if cur is None or cur[1] < v:
                acc[s.name] = (s, v)

        for k in reads:
            add(self.W.get(k))
        for k in writes:
            add(self.W.get(k))
            for sv in self.R.get(k, {}).values():
                add(sv)
        for sv in extra:
            add(sv)
        out = []
        for s, v in acc.values():
            assert v <= s.n, f"wait on future value of {s.name}: {v} > {s.n}"
            if self.waited[engine].get(s.name, 0) >= v:
                continue
            self.waited[engine][s.name] = v
            out.append((s, v))
        return out

    def op(self, engine, method, args, kwargs=None, reads=(), writes=(), extra=(),
           sem=None, amt=1, group=None):
        waits = self._waits(engine, reads, writes, extra)
        s = sem if sem is not None else self.esem[engine]
        s.n += amt
        v = s.n
        if group is None:
            group = [(method, args, kwargs or {})]
        self.ops[engine].append((waits, group, (s, amt)))
        for k in writes:
            self.W[k] = (s, v)
            self.R[k] = {}
        for k in reads:
            d = self.R.setdefault(k, {})
            cur = d.get(s.name)
            if cur is None or cur[1] < v:
                d[s.name] = (s, v)
        return (s, v)

    def dma(self, qeng, qsem, dst, src, reads=(), writes=(), extra=(), serialize=True, transpose=False):
        ex = list(extra)
        if qsem.n > 0:
            ex.append((qsem, qsem.n))
        method = "dma_start_transpose" if transpose else "dma_start"
        return self.op(qeng, method, (dst, src), None, reads, writes, ex, sem=qsem, amt=16)


def build_program(nc, B, SEQ):
    NST = SEQ // 256
    ROWS = B * SEQ
    em = Emitter()

    x_h = nc.dram_tensor("x", [ROWS, D], F32, kind="ExternalInput")
    wq_h = nc.dram_tensor("wq", [D, D], F32, kind="ExternalInput")
    wk_h = nc.dram_tensor("wk", [D, D], F32, kind="ExternalInput")
    wv_h = nc.dram_tensor("wv", [D, D], F32, kind="ExternalInput")
    w1_h = nc.dram_tensor("w1", [FF, D], F32, kind="ExternalInput")
    w2_h = nc.dram_tensor("w2", [D, FF], F32, kind="ExternalInput")
    bq_h = nc.dram_tensor("bq_l", [128, 8], F32, kind="ExternalInput")
    bk_h = nc.dram_tensor("bk_l", [128, 8], F32, kind="ExternalInput")
    b1_h = nc.dram_tensor("b1_l", [128, 32], F32, kind="ExternalInput")
    bv_h = nc.dram_tensor("bv_r", [1, D], F32, kind="ExternalInput")
    b2_h = nc.dram_tensor("b2_r", [1, D], F32, kind="ExternalInput")
    id_h = nc.dram_tensor("ident", [128, 128], BF16, kind="ExternalInput")
    tril_h = nc.dram_tensor("trilm", [128, 128], BF16, kind="ExternalInput")
    ones_h = nc.dram_tensor("onesbf", [1, 128], BF16, kind="ExternalInput")
    out_h = nc.dram_tensor("out", [ROWS, D], F32, kind="ExternalOutput")
    kthbm = nc.dram_tensor("kthbm", [B * NST * 8 * 128, 256], BF16, kind="Internal")
    w1tern_h = nc.dram_tensor("w1tern", [FF, D], BF16, kind="Internal")
    w2tern_h = nc.dram_tensor("w2tern", [D, FF], BF16, kind="Internal")
    w2t_h = nc.dram_tensor("w2t", [FF, D], BF16, kind="Internal")

    ctx = ExitStack()

    def sb(name, shape, dt):
        return ctx.enter_context(nc.sbuf_tensor(name, shape, dt))

    def psum(name, shape, dt):
        return ctx.enter_context(nc.psum_tensor(name, shape, dt))

    wq_sb = sb("wq_sb", [128, NJ * D], BF16)
    wk_sb = sb("wk_sb", [128, NJ * D], BF16)
    wv_sb = sb("wv_sb", [128, NJ * D], BF16)
    w1t_sb = sb("w1t_sb", [128, NJ * FF], BF16)
    w2buf = sb("w2buf", [128, 2 * 2048], BF16)
    ktbuf = sb("ktbuf", [128, 2 * 4096], BF16)
    ktcur = sb("ktcur", [128, 4096], BF16)
    xnt = sb("xnt", [128, 4096], BF16)
    qt = sb("qt", [128, 4096], BF16)
    xld = sb("xld", [128, 4096], F32)
    xnbf = sb("xnbf", [128, 2048], BF16)
    xbfres = sb("xbfres", [128, 4096], BF16)
    outst = sb("outst", [128, 2048], F32)
    hT = sb("hT", [128, 512], BF16)
    dump = sb("dump", [128, 2048], BF16)
    sts = sb("sts", [128, 56], F32)
    css = sb("css", [128, 80], F32)
    rr = sb("rr", [128, 8], F32)
    msk = sb("msk", [128, 128], BF16)
    bql = sb("bql", [128, 8], F32)
    bkl = sb("bkl", [128, 8], F32)
    b1l = sb("b1l", [128, 32], F32)
    bvb2 = sb("bvb2", [1, 2048], BF16)
    onesbf = sb("ones_sb", [1, 128], BF16)
    onesf = sb("onesf", [1, 128], F32)
    onescol = sb("onescol", [128, 1], F32)
    ident = sb("ident_sb", [128, 128], BF16)
    trilm = sb("trilm_sb", [128, 128], BF16)
    acc1 = sb("acc1", [128, 16], F32)
    acc2 = acc1

    thrbc = sts[:, 48:52]
    thrpack = sts[0:1, 52:56]

    ps_qk = psum("ps_qk", [128, 1024], F32)
    ps_s = psum("ps_s", [128, 1024], F32)
    ps_o2a = psum("ps_o2a", [128, 1024], F32)
    ps_o2b = psum("ps_o2b", [128, 1024], F32)
    ps_o2 = (ps_o2a, ps_o2b)
    psqk_bf = ps_qk[:].bitcast(BF16)
    pss_bf = ps_s[:].bitcast(BF16)

    sV = em.sem("sV")
    sA = em.sem("sA")
    sP = em.sem("sP")
    sL = em.sem("sL")
    s_x = [em.sem("s_x0"), em.sem("s_x1")]
    s_kt = [em.sem("s_kt0"), em.sem("s_kt1")]
    s_w2 = [em.sem("s_w20"), em.sem("s_w21")]
    s_pl = [em.sem(f"s_pl{i}") for i in range(4)]
    s_tp = [em.sem("s_tp0"), em.sem("s_tp1")]
    s_ktw = em.sem("s_ktw")
    s_ow = [em.sem("s_ow0"), em.sem("s_ow1")]
    s_gp = [em.sem(f"s_gp{i}") for i in range(4)]
    s_wl = em.sem("s_wl")
    em.esem = {"vector": sV, "scalar": sA, "tensor": sP}

    E = em.op
    DMA = em.dma

    _slots = {}
    _next = [0]

    def slot(name):
        if name not in _slots:
            _slots[name] = _next[0]
            _next[0] += 1
            assert _next[0] <= 48
        c = _slots[name]
        return sts[:, c:c + 1], f"sts:{name}"

    DMA("sync", sL, bql[:], bq_h.ap(), writes=["bql"])
    DMA("sync", sL, bkl[:], bk_h.ap(), writes=["bkl"])
    DMA("sync", sL, b1l[:], b1_h.ap(), writes=["b1l"])
    DMA("sync", sL, ident[:], id_h.ap(), writes=["ident"])
    DMA("sync", sL, trilm[:], tril_h.ap(), writes=["trilm"])
    DMA("sync", sL, onesbf[:], ones_h.ap(), writes=["onesbf"])
    DMA("sync", sL, xld[0:1, 0:1024], bv_h.ap(), writes=["xld00"])
    DMA("sync", sL, xld[0:1, 1024:2048], b2_h.ap(), writes=["xld01"])
    E("vector", "tensor_copy", (bvb2[0:1, 0:1024], xld[0:1, 0:1024]), reads=["xld00"], writes=["bvb2"])
    E("vector", "tensor_copy", (bvb2[0:1, 1024:2048], xld[0:1, 1024:2048]), reads=["xld01"], writes=["bvb2"])
    E("vector", "memset", (onescol[:], 1.0), writes=["onescol"])
    E("vector", "memset", (onesf[:], 1.0), writes=["onesf"])

    for w_hh, w_ss, key in ((wq_h, wq_sb, "wq"), (wk_h, wk_sb, "wk"), (wv_h, wv_sb, "wv")):
        DMA("gpsimd", s_wl,
            w_ss[:].rearrange("p (ki c) -> p ki c", ki=NJ),
            w_hh.ap().rearrange("(ki p) c -> p ki c", p=128),
            writes=[key])

    pbuf = (outst, None)
    pkeys = (["outst:0", "outst:1"], ["xld10", "xld11"])

    def emit_prep():
        def pB(par):
            return outst[:] if par == 0 else xld[:, 2048:4096]
        for tt in range(16):
            buf, bkey = pB(tt % 2), pkeys[tt % 2]
            DMA("sync", s_pl[tt % 4],
                buf.rearrange("p (two c) -> p two c", two=2),
                w1_h.ap()[tt * 256:(tt + 1) * 256, :].rearrange("(two p) c -> p two c", p=128),
                writes=bkey)
            E("scalar", "activation", (dump[:, 0:2048], buf, AF.Abs),
              {"accum_out": acc1[:, tt:tt + 1]},
              reads=bkey, writes=["dump:0", "dump:1", f"acc:{tt}"])
        cs1, cs1k = slot("colsum1")
        E("vector", "reduce_sum", (cs1, acc1[:, 0:16]), {"axis": AX.X},
          reads=[f"acc:{t}" for t in range(16)], writes=[cs1k])
        E("tensor", "matmul", (ps_s[0:1, 0:1], cs1, onescol[:]), {"start": True, "stop": True},
          reads=[cs1k, "onescol"], writes=["pss:0"])
        NW = float(FF * D)
        E("vector", "tensor_scalar", (thrpack[0:1, 0:1], ps_s[0:1, 0:1], 0.5 / NW, 0.5e-8, ALU.mult, ALU.max),
          reads=["pss:0"], writes=["thrpack"])
        E("vector", "tensor_scalar", (thrpack[0:1, 1:2], thrpack[0:1, 0:1], -1.0, None, ALU.mult),
          reads=["thrpack"], writes=["thrpack"])
        E("tensor", "matmul", (ps_s[:, 8:10], onesf[:], thrpack[0:1, 0:2]), {"start": True, "stop": True},
          reads=["thrpack", "onesf"], writes=["pss:0"])
        E("vector", "tensor_copy", (thrbc[:, 0:2], ps_s[:, 8:10]), reads=["pss:0"], writes=["thrbc"])
        t1v, t1k = w2buf[:, 0:2048], ["w2buf:0"]
        t2v, t2k = w2buf[:, 2048:4096], ["w2buf:1"]

        def tern_tile(buf, bkey, thr_c, nthr_c, par):
            if par == 0:
                wtv, wtk = w2buf[:, 0:2048], ["w2buf:0"]
            else:
                wtv, wtk = dump[:, 0:2048], ["dump:0", "dump:1"]
            E("vector", "tensor_scalar", (t2v, buf, thrbc[:, nthr_c:nthr_c + 1], None, ALU.is_lt),
              reads=[bkey, "thrbc"], writes=t2k)
            E("vector", "scalar_tensor_tensor",
              (wtv, buf, thrbc[:, thr_c:thr_c + 1], t2v, ALU.is_gt, ALU.subtract),
              reads=[bkey, "thrbc"] + t2k, writes=wtk)
            return wtv, wtk

        for tt in range(16):
            buf, bkey = pB(tt % 2), pkeys[tt % 2][0:2]
            DMA("sync", s_pl[tt % 4],
                buf.rearrange("p (two c) -> p two c", two=2),
                w1_h.ap()[tt * 256:(tt + 1) * 256, :].rearrange("(two p) c -> p two c", p=128),
                writes=bkey)
            wtv, wtk = tern_tile(buf, bkey[0], 0, 1, tt % 2)
            tbf = psqk_bf if tt % 2 == 0 else pss_bf
            tkeys = ["psqk:0", "psqk:1"] if tt % 2 == 0 else ["pss:0", "pss:1"]
            grp = []
            for u in range(2):
                for j in range(NJ):
                    k = u * NJ + j
                    grp.append(("transpose",
                                (tbf[:, k * 128:(k + 1) * 128],
                                 wtv[:, u * 1024 + j * 128:u * 1024 + (j + 1) * 128],
                                 ident[:]), {}))
            E("tensor", None, None, group=grp, reads=wtk + ["ident"], writes=tkeys)
            for u in range(2):
                E("scalar", "copy",
                  (w1t_sb[:].rearrange("p (j f) -> p j f", j=NJ)[:, :, (tt * 2 + u) * 128:(tt * 2 + u + 1) * 128],
                   tbf[:, u * 1024:(u + 1) * 1024].rearrange("p (j c) -> p j c", j=NJ)),
                  reads=tkeys, writes=["w1t"])

        ktf32 = ktbuf[:].bitcast(F32)
        for tt in range(16):
            j, qq = divmod(tt, 2)
            buf2 = ktf32[:, (tt % 2) * 2048:(tt % 2 + 1) * 2048]
            bkey2 = [f"ktbuf:{tt % 2}"]
            DMA("gpsimd", s_gp[tt % 4], buf2,
                w2_h.ap()[j * 128:(j + 1) * 128, qq * 2048:(qq + 1) * 2048],
                writes=bkey2)
            E("scalar", "activation", (dump[:, 0:2048], buf2, AF.Abs),
              {"accum_out": acc2[:, tt:tt + 1]},
              reads=bkey2, writes=["dump:0", "dump:1", f"acc:{tt}"])

        cs2, cs2k = slot("colsum2")
        E("vector", "reduce_sum", (cs2, acc2[:, 0:16]), {"axis": AX.X},
          reads=[f"acc:{t}" for t in range(16)], writes=[cs2k])
        E("tensor", "matmul", (ps_s[0:1, 4:5], cs2, onescol[:]), {"start": True, "stop": True},
          reads=[cs2k, "onescol"], writes=["pss:0"])
        NW2 = float(FF * D)
        E("vector", "tensor_scalar", (thrpack[0:1, 2:3], ps_s[0:1, 4:5], 0.5 / NW2, 0.5e-8, ALU.mult, ALU.max),
          reads=["pss:0"], writes=["thrpack"])
        E("vector", "tensor_scalar", (thrpack[0:1, 3:4], thrpack[0:1, 2:3], -1.0, None, ALU.mult),
          reads=["thrpack"], writes=["thrpack"])
        E("tensor", "matmul", (ps_s[:, 10:12], onesf[:], thrpack[0:1, 2:4]), {"start": True, "stop": True},
          reads=["thrpack", "onesf"], writes=["pss:0"])
        E("vector", "tensor_copy", (thrbc[:, 2:4], ps_s[:, 10:12]), reads=["pss:0"], writes=["thrbc"])
        for qq2 in range(2):
            for j in range(NJ):
                tt = qq2 * 8 + j
                buf, bkey = pB(tt % 2), pkeys[tt % 2][0:2]
                DMA("sync", s_pl[tt % 4], buf,
                    w2_h.ap()[j * 128:(j + 1) * 128, qq2 * 2048:(qq2 + 1) * 2048],
                    writes=bkey)
                wtv, wtk = tern_tile(buf, bkey[0], 2, 3, tt % 2)
                tbf = psqk_bf if tt % 2 == 0 else pss_bf
                tkeys = ["psqk:0", "psqk:1"] if tt % 2 == 0 else ["pss:0", "pss:1"]
                grp = []
                for fs in range(16):
                    grp.append(("transpose",
                                (tbf[:, fs * 128:(fs + 1) * 128],
                                 wtv[:, fs * 128:(fs + 1) * 128],
                                 ident[:]), {}))
                E("tensor", None, None, group=grp, reads=wtk + ["ident"], writes=tkeys)
                stage = ktbuf[:, (j % 2) * 4096:(j % 2) * 4096 + 2048]
                skey = f"ktbuf:{j % 2}"
                E("scalar", "copy", (stage, tbf[:, 0:2048]),
                  reads=tkeys, writes=[skey])
                DMA("gpsimd", s_gp[tt % 4],
                    w2t_h.ap()[qq2 * 2048:(qq2 + 1) * 2048, j * 128:(j + 1) * 128].rearrange(
                        "(fs p) c -> p fs c", p=128),
                    stage.rearrange("p (fs c) -> p fs c", fs=16),
                    reads=[skey], writes=[f"w2t:{qq2}:{j}"])

    NSUP = B * NST

    _dump_ctr = [0]

    def em_dump_next():
        _dump_ctr[0] ^= 1
        return _dump_ctr[0]

    def emit_x_loads(sup):
        b, g = divmod(sup, NST)
        p = sup % 2
        for i in range(2):
            r0 = b * SEQ + g * 256 + i * 128
            buf = xld[:, p * 2048 + i * 1024:p * 2048 + (i + 1) * 1024]
            DMA("sync", s_x[i], buf, x_h.ap()[r0:r0 + 128, :], writes=[f"xld{p}{i}"])

    def emit_ln_stats(sup):
        p = sup % 2
        bufs, bkeys, sl = [], [], []
        for i in range(2):
            bufs.append(xld[:, p * 2048 + i * 1024:p * 2048 + (i + 1) * 1024])
            bkeys.append(f"xld{p}{i}")
            sl.append({nm: slot(f"{nm}{i}_{p}") for nm in
                       ("mean0", "mean", "sumsq", "vp0", "msq", "vpe", "lnv", "rstd")})
        for i in range(2):
            E("vector", "reduce_sum", (sl[i]["mean0"][0], bufs[i]), {"axis": AX.X},
              reads=[bkeys[i]], writes=[sl[i]["mean0"][1]])
        for i in range(2):
            dpar = em_dump_next()
            E("scalar", "activation", (dump[:, dpar * 1024:dpar * 1024 + 1024], bufs[i], AF.Square),
              {"accum_out": sl[i]["sumsq"][0]},
              reads=[bkeys[i]], writes=[f"dump:{dpar}", sl[i]["sumsq"][1]])
        for i in range(2):
            E("vector", "tensor_scalar", (sl[i]["mean"][0], sl[i]["mean0"][0], 1.0 / D, None, ALU.mult),
              reads=[sl[i]["mean0"][1]], writes=[sl[i]["mean"][1]])
        for i in range(2):
            E("vector", "tensor_scalar", (sl[i]["vp0"][0], sl[i]["sumsq"][0], 1.0 / D, 1e-5, ALU.mult, ALU.add),
              reads=[sl[i]["sumsq"][1]], writes=[sl[i]["vp0"][1]])
        for i in range(2):
            E("vector", "tensor_tensor", (sl[i]["msq"][0], sl[i]["mean"][0], sl[i]["mean"][0], ALU.mult),
              reads=[sl[i]["mean"][1]], writes=[sl[i]["msq"][1]])
        for i in range(2):
            E("vector", "tensor_tensor", (sl[i]["vpe"][0], sl[i]["vp0"][0], sl[i]["msq"][0], ALU.subtract),
              reads=[sl[i]["vp0"][1], sl[i]["msq"][1]], writes=[sl[i]["vpe"][1]])
        for i in range(2):
            E("scalar", "activation", (sl[i]["lnv"][0], sl[i]["vpe"][0], AF.Ln),
              reads=[sl[i]["vpe"][1]], writes=[sl[i]["lnv"][1]])
        for i in range(2):
            E("scalar", "activation", (sl[i]["rstd"][0], sl[i]["lnv"][0], AF.Exp), {"scale": -0.5},
              reads=[sl[i]["lnv"][1]], writes=[sl[i]["rstd"][1]])

    def emit_ln_finish(sup):
        p = sup % 2
        for i in range(2):
            buf = xld[:, p * 2048 + i * 1024:p * 2048 + (i + 1) * 1024]
            bkey = f"xld{p}{i}"
            mean, meank = slot(f"mean{i}_{p}")
            rstd, rstdk = slot(f"rstd{i}_{p}")
            E("vector", "tensor_scalar",
              (xnbf[:, i * 1024:(i + 1) * 1024], buf, mean, rstd, ALU.subtract, ALU.mult),
              reads=[bkey, meank, rstdk], writes=[f"xnbf:{i}"])
            E("vector", "tensor_copy",
              (xbfres[:, p * 2048 + i * 1024:p * 2048 + (i + 1) * 1024], buf),
              reads=[bkey], writes=[f"xbfres:{p}:{i}"])

    attn_terms = {}

    def emit_rfinal(u):
        NPB = NST // 2
        b, gp = divmod(u, NPB)
        pp = u % 2
        for qp in range(4):
            n = attn_terms[u][qp]
            t_loc = 4 * gp + qp
            cnt = float(SEQ - 128 * (t_loc + 1))
            zs, zsk = slot(f"zs_{qp}")
            zf, zfk = slot(f"zf_{qp}")
            rz, rzk = slot(f"rz_{qp}")
            ckeys = [f"css:{pp}:{qp}:{i}" for i in range(n)]
            E("vector", "reduce_sum", (zs, css[:, pp * 40 + qp * 10:pp * 40 + qp * 10 + n]), {"axis": AX.X},
              reads=ckeys, writes=[zsk])
            E("vector", "tensor_scalar", (zf, zs, cnt, None, ALU.add),
              reads=[zsk], writes=[zfk])
            E("vector", "reciprocal", (rz, zf), reads=[zfk], writes=[rzk])
            E("vector", "tensor_tensor", (rr[:, pp * 4 + qp:pp * 4 + qp + 1], zf, rz, ALU.mult),
              reads=[zfk, rzk], writes=[f"rr:{pp}:{qp}"])

    def emit_attn_pair(u):
        NPB = NST // 2
        b, gp = divmod(u, NPB)
        pp = u % 2
        cbase = pp * 40

        for half in range(2):
            sup = 2 * u + half
            if half == 1:
                emit_ln_finish(sup)
            p = sup % 2
            xntv = xnt[:, p * 2048:(p + 1) * 2048]
            for i in range(2):
                grp = []
                for j in range(NJ):
                    grp.append(("transpose",
                                (psqk_bf[:, i * 1024 + j * 128:i * 1024 + (j + 1) * 128],
                                 xnbf[:, i * 1024 + j * 128:i * 1024 + (j + 1) * 128],
                                 ident[:]), {}))
                E("tensor", None, None, group=grp,
                  reads=[f"xnbf:{i}", "ident"], writes=[f"psqk:{i}"])
                E("vector", "tensor_copy",
                  (xntv.rearrange("p (j c) -> p j c", j=NJ)[:, :, i * 128:(i + 1) * 128],
                   psqk_bf[:, i * 1024:(i + 1) * 1024].rearrange("p (j c) -> p j c", j=NJ)),
                  reads=[f"psqk:{i}"], writes=[f"xnt:{p}"])

        for wsb, wkey, bias_sb, bias_key, dest, destkey in (
                (wq_sb, "wq", bql, "bql", qt, "qt"),
                (wk_sb, "wk", bkl, "bkl", ktcur, "ktcur")):
            for mj in range(NJ):
                bank = mj % 2
                grp = []
                for half in range(2):
                    xntv = xnt[:, (2 * u + half) % 2 * 2048:((2 * u + half) % 2 + 1) * 2048]
                    ps = ps_qk[:, bank * 512 + half * 256:bank * 512 + (half + 1) * 256]
                    for ki in range(NJ):
                        grp.append(("matmul",
                                    (ps, wsb[:, ki * D + mj * 128:ki * D + (mj + 1) * 128],
                                     xntv[:, ki * 256:(ki + 1) * 256]),
                                    {"start": half == 0 and ki == 0,
                                     "stop": half == 1 and ki == NJ - 1}))
                E("tensor", None, None, group=grp,
                  reads=[wkey, "xnt:0", "xnt:1"], writes=[f"psqk:{bank}"])
                E("scalar", "activation",
                  (dest[:, mj * 512:(mj + 1) * 512], ps_qk[:, bank * 512:(bank + 1) * 512],
                   AF.Identity),
                  {"bias": bias_sb[:, mj:mj + 1]},
                  reads=[f"psqk:{bank}", bias_key], writes=[destkey])

        for half in range(2):
            base = (b * NST + 2 * gp + half) * 8 * 128
            DMA("gpsimd", s_ktw,
                kthbm.ap()[base:base + 1024, :].rearrange("(j p) c -> p j c", p=128),
                ktcur[:].rearrange("p (j c) -> p j c", j=NJ)[:, :, half * 256:(half + 1) * 256],
                reads=["ktcur"], writes=[f"kthbm:{b}:{2 * gp + half}"])

        nch = gp

        def load_ktchunk(c):
            cb = c % 2
            for gg in range(2):
                kb = (b * NST + 2 * c + gg) * 8 * 128
                DMA("sync", s_kt[cb],
                    ktbuf[:, cb * 4096:(cb + 1) * 4096].rearrange(
                        "p (j gg c) -> p gg j c", j=NJ, gg=2)[:, gg],
                    kthbm.ap()[kb:kb + 1024, :].rearrange("(j p) c -> p j c", p=128),
                    reads=[f"kthbm:{b}:{2 * c + gg}"],
                    writes=[f"ktbuf:{cb}"])

        for c in range(min(nch, 2)):
            load_ktchunk(c)

        state = [{"n": 0} for _ in range(4)]

        def term(qp, src_ap, src_keys, width):
            st = state[qp]
            idx = st["n"]
            st["n"] += 1
            cs = css[:, cbase + qp * 10 + idx:cbase + qp * 10 + idx + 1]
            csk = f"css:{pp}:{qp}:{idx}"
            dpar = em_dump_next()
            E("scalar", "activation",
              (dump[:, dpar * 1024:dpar * 1024 + width], src_ap, AF.Exp),
              {"scale": 1.0 / 32.0, "accum_out": cs},
              reads=src_keys, writes=[f"dump:{dpar}", csk])

        SB4 = ((ps_s, 0, "pss:0"), (ps_s, 512, "pss:1"),
               (ps_qk, 0, "psqk:0"), (ps_qk, 512, "psqk:1"))

        for c in range(nch):
            cb = c % 2
            for qp in range(4):
                pst, off, bkey_ps = SB4[qp]
                ps = pst[:, off:off + 512]
                grp = []
                for j in range(NJ):
                    grp.append(("matmul",
                                (ps, qt[:, j * 512 + qp * 128:j * 512 + (qp + 1) * 128],
                                 ktbuf[:, cb * 4096 + j * 512:cb * 4096 + (j + 1) * 512]),
                                {"start": j == 0, "stop": j == NJ - 1}))
                E("tensor", None, None, group=grp,
                  reads=["qt", f"ktbuf:{cb}"], writes=[bkey_ps])
                term(qp, ps, [bkey_ps], 512)
            if c + 2 < nch:
                load_ktchunk(c + 2)
        for qp in range(1, 4):
            rem = qp * 128
            start = 0
            while rem > 0:
                w = min(256, rem)
                pst, off, bkey_ps = SB4[qp]
                ps = pst[:, off:off + w]
                grp = []
                for j in range(NJ):
                    grp.append(("matmul",
                                (ps, qt[:, j * 512 + qp * 128:j * 512 + (qp + 1) * 128],
                                 ktcur[:, j * 512 + start:j * 512 + start + w]),
                                {"start": j == 0, "stop": j == NJ - 1}))
                E("tensor", None, None, group=grp, reads=["qt", "ktcur"], writes=[bkey_ps])
                term(qp, ps, [bkey_ps], w)
                start += w
                rem -= w
        for qp in range(4):
            pst, off, bkey_ps = SB4[qp]
            ps = pst[:, off:off + 128]
            grp = []
            for j in range(NJ):
                grp.append(("matmul",
                            (ps, qt[:, j * 512 + qp * 128:j * 512 + (qp + 1) * 128],
                             ktcur[:, j * 512 + qp * 128:j * 512 + (qp + 1) * 128]),
                            {"start": j == 0, "stop": j == NJ - 1}))
            E("tensor", None, None, group=grp, reads=["qt", "ktcur"], writes=[bkey_ps])
            mv = msk[:, 0:128]
            E("vector", "tensor_tensor", (mv, ps, trilm[:], ALU.mult),
              reads=[bkey_ps, "trilm"], writes=["msk"])
            term(qp, mv, ["msk"], 128)
        attn_terms[u] = tuple(state[qp]["n"] for qp in range(4))
        emit_rfinal(u)

    def emit_ffn(sup):
        b, g = divmod(sup, NST)
        p = sup % 2
        pp = (sup // 2) % 2
        xntv = xnt[:, p * 2048:(p + 1) * 2048]
        xntk = f"xnt:{p}"

        def load_w2chunk(k):
            wb = k % 2
            DMA("sync", s_w2[wb],
                w2buf[:, wb * 2048:(wb + 1) * 2048].rearrange("p (ft c) -> p ft c", ft=2),
                w2t_h.ap()[k * 256:(k + 1) * 256, :].rearrange("(ft p) c -> p ft c", p=128),
                reads=[f"w2t:{k // 8}:{j}" for j in range(NJ)], writes=[f"w2buf:{wb}"])

        load_w2chunk(0)
        load_w2chunk(1)
        def emit_h(f):
            bank = f % 2
            ps = ps_qk[:, bank * 512:bank * 512 + 256]
            grp = []
            for ki in range(NJ):
                grp.append(("matmul",
                            (ps, w1t_sb[:, ki * FF + f * 128:ki * FF + (f + 1) * 128],
                             xntv[:, ki * 256:(ki + 1) * 256]),
                            {"start": ki == 0, "stop": ki == NJ - 1}))
            E("tensor", None, None, group=grp, reads=["w1t", xntk], writes=[f"psqk:{bank}"])
            E("scalar", "activation",
              (hT[:, bank * 256:(bank + 1) * 256], ps, AF.Gelu),
              {"bias": b1l[:, f:f + 1]},
              reads=[f"psqk:{bank}", "b1l"], writes=[f"hT:{bank}"])

        def emit_o2(f):
            bank = f % 2
            wb = (f // 2) % 2
            grp = []
            reads = [f"hT:{bank}", f"w2buf:{wb}"]
            if f == 0:
                for s in range(2):
                    for half in range(2):
                        grp.append(("matmul",
                                    (ps_o2[s][:, half * 512:(half + 1) * 512],
                                     onesbf[:],
                                     bvb2[0:1, 1024 + half * 512:1024 + (half + 1) * 512]),
                                    {"start": True, "stop": False}))
                reads += ["onesbf", "bvb2"]
            for s in range(2):
                for half in range(2):
                    grp.append(("matmul",
                                (ps_o2[s][:, half * 512:(half + 1) * 512],
                                 hT[:, bank * 256 + s * 128:bank * 256 + (s + 1) * 128],
                                 w2buf[:, wb * 2048 + (f % 2) * 1024 + half * 512:
                                       wb * 2048 + (f % 2) * 1024 + (half + 1) * 512]),
                                {"start": False, "stop": f == NFT - 1}))
            E("tensor", None, None, group=grp, reads=reads,
              writes=(["pso2a", "pso2b"] if (f == 0 or f == NFT - 1) else []),
              extra=([] if f == 0 else [em.W["pso2a"], em.W["pso2b"]]))

        def emit_v(s):
            grp = []
            for half in range(2):
                ps = ps_s[:, half * 512:(half + 1) * 512]
                grp.append(("matmul",
                            (ps, onesbf[:], bvb2[0:1, half * 512:(half + 1) * 512]),
                            {"start": True, "stop": False}))
                for ki in range(NJ):
                    grp.append(("matmul",
                                (ps, xntv[:, ki * 256 + s * 128:ki * 256 + (s + 1) * 128],
                                 wv_sb[:, ki * D + half * 512:ki * D + (half + 1) * 512]),
                                {"start": False, "stop": ki == NJ - 1}))
            E("tensor", None, None, group=grp,
              reads=["onesbf", "bvb2", xntk, "wv"], writes=["pss:0", "pss:1"])
            qp = (sup % 2) * 2 + s
            E("vector", "scalar_tensor_tensor",
              (outst[:, s * 1024:(s + 1) * 1024], ps_s[:, 0:1024], rr[:, pp * 4 + qp:pp * 4 + qp + 1],
               xbfres[:, p * 2048 + s * 1024:p * 2048 + (s + 1) * 1024],
               ALU.mult, ALU.add),
              reads=["pss:0", "pss:1", f"rr:{pp}:{qp}", f"xbfres:{p}:{s}"],
              writes=[f"outst:{s}"])

        emit_h(0)
        emit_h(1)
        emit_v(0)
        for f in range(2, NFT):
            emit_o2(f - 2)
            if (f - 2) % 2 == 1 and (f - 2) // 2 + 2 < 16:
                load_w2chunk((f - 2) // 2 + 2)
            if f == 4:
                emit_v(1)
            emit_h(f)
        emit_o2(NFT - 2)
        emit_o2(NFT - 1)
        if sup + 2 < NSUP:
            emit_ln_stats(sup + 2)

        for s in range(2):
            E("vector", "scalar_tensor_tensor",
              (outst[:, s * 1024:(s + 1) * 1024], ps_o2[s][:, 0:1024], 0.0,
               outst[:, s * 1024:(s + 1) * 1024], ALU.bypass, ALU.add),
              reads=[("pso2a" if s == 0 else "pso2b"), f"outst:{s}"],
              writes=[f"outst:{s}"])
            r0 = b * SEQ + g * 256 + s * 128
            DMA("gpsimd", s_ow[s], out_h.ap()[r0:r0 + 128, :], outst[:, s * 1024:(s + 1) * 1024],
                reads=[f"outst:{s}"], writes=[])

    NP = NSUP // 2
    emit_x_loads(0)
    emit_ln_stats(0)
    emit_x_loads(1)
    emit_ln_stats(1)
    emit_ln_finish(0)
    emit_attn_pair(0)
    if NSUP > 2:
        emit_x_loads(2)
    emit_prep()
    for u in range(1, NP):
        emit_ffn(2 * u - 2)
        if 2 * u + 1 < NSUP:
            emit_x_loads(2 * u + 1)
        emit_ln_finish(2 * u)
        emit_ffn(2 * u - 1)
        if 2 * u + 2 < NSUP:
            emit_x_loads(2 * u + 2)
        emit_attn_pair(u)
    emit_ffn(NSUP - 2)
    emit_ffn(NSUP - 1)

    em.ops["gpsimd"].append(([(s_ow[0], s_ow[0].n), (s_ow[1], s_ow[1].n), (s_ktw, s_ktw.n)],
                             [], (None, 0)))

    for s in em.sems:
        s.h = ctx.enter_context(nc.semaphore(s.name))
    blk = ctx.enter_context(nc.Block())

    def runner(lst):
        def go(eng):
            for waits, group, (s, amt) in lst:
                for ws, wv in waits:
                    eng.wait_ge(ws.h, wv)
                last = None
                for method, args, kwargs in group:
                    last = getattr(eng, method)(*args, **kwargs)
                if last is not None and s is not None:
                    last.then_inc(s.h, amt)
        return go

    blk.sync(runner(em.ops["sync"]))
    blk.gpsimd(runner(em.ops["gpsimd"]))
    blk.vector(runner(em.ops["vector"]))
    blk.scalar(runner(em.ops["scalar"]))
    blk.tensor(runner(em.ops["tensor"]))
    ctx.close()
    return nc


def _host_inputs(L0_attn_Wq, L0_attn_bq, L0_attn_Wk, L0_attn_bk, L0_attn_Wv, L0_attn_bv,
                 L0_ffn_W1, L0_ffn_b1, L0_ffn_W2, L0_ffn_b2):
    f32 = lambda a: np.ascontiguousarray(np.asarray(a, dtype=np.float32))
    common = {
        "wq": f32(L0_attn_Wq), "wk": f32(L0_attn_Wk), "wv": f32(L0_attn_Wv),
        "w1": f32(L0_ffn_W1), "w2": f32(L0_ffn_W2),
        "bq_l": f32(np.asarray(L0_attn_bq).reshape(8, 128).T),
        "bk_l": f32(np.asarray(L0_attn_bk).reshape(8, 128).T),
        "b1_l": f32(np.asarray(L0_ffn_b1).reshape(32, 128).T),
        "bv_r": f32(np.asarray(L0_attn_bv).reshape(1, D)),
        "b2_r": f32(np.asarray(L0_ffn_b2).reshape(1, D)),
        "ident": np.eye(128, dtype=ml_dtypes.bfloat16),
        "trilm": np.tril(np.ones((128, 128), dtype=ml_dtypes.bfloat16)),
        "onesbf": np.ones((1, 128), dtype=ml_dtypes.bfloat16),
    }
    return common


def kernel(L0_attn_X, L0_attn_Wq, L0_attn_bq, L0_attn_Wk, L0_attn_bk,
           L0_attn_Wv, L0_attn_bv, L0_ffn_W1, L0_ffn_b1, L0_ffn_W2, L0_ffn_b2):
    NB, SEQ, _ = L0_attn_X.shape
    ncores = 8
    bpc = NB // ncores
    X = np.ascontiguousarray(np.asarray(L0_attn_X, dtype=np.float32)).reshape(NB * SEQ, D)
    common = _host_inputs(L0_attn_Wq, L0_attn_bq, L0_attn_Wk, L0_attn_bk, L0_attn_Wv,
                          L0_attn_bv, L0_ffn_W1, L0_ffn_b1, L0_ffn_W2, L0_ffn_b2)
    nc = bass.Bass("TRN2", target_bir_lowering=False)
    build_program(nc, bpc, SEQ)
    in_maps = []
    for c in range(ncores):
        m = dict(common)
        m["x"] = np.ascontiguousarray(X[c * bpc * SEQ:(c + 1) * bpc * SEQ, :])
        in_maps.append(m)
    res = run_bass_kernel_spmd(nc, in_maps, core_ids=list(range(ncores)))
    out = np.concatenate([np.asarray(r["out"]) for r in res.results], axis=0)
    return out.reshape(NB, SEQ, D)


# version 23
# speedup vs baseline: 1.0870x; 1.0870x over previous
import numpy as np
import ml_dtypes
from contextlib import ExitStack

import concourse.bass as bass
import concourse.mybir as mybir
from concourse.bass_utils import run_bass_kernel_spmd

F32 = mybir.dt.float32
BF16 = mybir.dt.bfloat16
AF = mybir.ActivationFunctionType
ALU = mybir.AluOpType
AX = mybir.AxisListType

D = 1024
FF = 4096
NJ = 8
NFT = 32
ENGS = ("sync", "gpsimd", "vector", "scalar", "tensor")


class Sem:
    __slots__ = ("name", "h", "n")

    def __init__(self, name):
        self.name = name
        self.h = None
        self.n = 0


class Emitter:
    def __init__(self):
        self.ops = {e: [] for e in ENGS}
        self.sems = []
        self.esem = {}
        self.waited = {e: {} for e in ENGS}
        self.W = {}
        self.R = {}

    def sem(self, name):
        s = Sem(name)
        self.sems.append(s)
        return s

    def _waits(self, engine, reads, writes, extra):
        acc = {}

        def add(sv):
            if sv is None:
                return
            s, v = sv
            if v <= 0:
                return
            cur = acc.get(s.name)
            if cur is None or cur[1] < v:
                acc[s.name] = (s, v)

        for k in reads:
            add(self.W.get(k))
        for k in writes:
            add(self.W.get(k))
            for sv in self.R.get(k, {}).values():
                add(sv)
        for sv in extra:
            add(sv)
        out = []
        for s, v in acc.values():
            assert v <= s.n, f"wait on future value of {s.name}: {v} > {s.n}"
            if self.waited[engine].get(s.name, 0) >= v:
                continue
            self.waited[engine][s.name] = v
            out.append((s, v))
        return out

    def op(self, engine, method, args, kwargs=None, reads=(), writes=(), extra=(),
           sem=None, amt=1, group=None):
        waits = self._waits(engine, reads, writes, extra)
        s = sem if sem is not None else self.esem[engine]
        s.n += amt
        v = s.n
        if group is None:
            group = [(method, args, kwargs or {})]
        self.ops[engine].append((waits, group, (s, amt)))
        for k in writes:
            self.W[k] = (s, v)
            self.R[k] = {}
        for k in reads:
            d = self.R.setdefault(k, {})
            cur = d.get(s.name)
            if cur is None or cur[1] < v:
                d[s.name] = (s, v)
        return (s, v)

    def dma(self, qeng, qsem, dst, src, reads=(), writes=(), extra=(), serialize=True, transpose=False):
        ex = list(extra)
        if qsem.n > 0:
            ex.append((qsem, qsem.n))
        method = "dma_start_transpose" if transpose else "dma_start"
        return self.op(qeng, method, (dst, src), None, reads, writes, ex, sem=qsem, amt=16)


def build_program(nc, B, SEQ):
    NST = SEQ // 256
    ROWS = B * SEQ
    em = Emitter()

    x_h = nc.dram_tensor("x", [ROWS, D], F32, kind="ExternalInput")
    wq_h = nc.dram_tensor("wq", [D, D], F32, kind="ExternalInput")
    wk_h = nc.dram_tensor("wk", [D, D], F32, kind="ExternalInput")
    wv_h = nc.dram_tensor("wv", [D, D], F32, kind="ExternalInput")
    w1_h = nc.dram_tensor("w1", [FF, D], F32, kind="ExternalInput")
    w2_h = nc.dram_tensor("w2", [D, FF], F32, kind="ExternalInput")
    bq_h = nc.dram_tensor("bq_l", [128, 8], F32, kind="ExternalInput")
    bk_h = nc.dram_tensor("bk_l", [128, 8], F32, kind="ExternalInput")
    b1_h = nc.dram_tensor("b1_l", [128, 32], F32, kind="ExternalInput")
    bv_h = nc.dram_tensor("bv_r", [1, D], F32, kind="ExternalInput")
    b2_h = nc.dram_tensor("b2_r", [1, D], F32, kind="ExternalInput")
    id_h = nc.dram_tensor("ident", [128, 128], BF16, kind="ExternalInput")
    tril_h = nc.dram_tensor("trilm", [128, 128], BF16, kind="ExternalInput")
    ones_h = nc.dram_tensor("onesbf", [1, 128], BF16, kind="ExternalInput")
    out_h = nc.dram_tensor("out", [ROWS, D], F32, kind="ExternalOutput")
    kthbm = nc.dram_tensor("kthbm", [B * NST * 8 * 128, 256], BF16, kind="Internal")
    w1tern_h = nc.dram_tensor("w1tern", [FF, D], BF16, kind="Internal")
    w2tern_h = nc.dram_tensor("w2tern", [D, FF], BF16, kind="Internal")
    w2t_h = nc.dram_tensor("w2t", [FF, D], BF16, kind="Internal")

    ctx = ExitStack()

    def sb(name, shape, dt):
        return ctx.enter_context(nc.sbuf_tensor(name, shape, dt))

    def psum(name, shape, dt):
        return ctx.enter_context(nc.psum_tensor(name, shape, dt))

    wq_sb = sb("wq_sb", [128, NJ * D], BF16)
    wk_sb = sb("wk_sb", [128, NJ * D], BF16)
    wv_sb = sb("wv_sb", [128, NJ * D], BF16)
    w1t_sb = sb("w1t_sb", [128, NJ * FF], BF16)
    w2buf = sb("w2buf", [128, 2 * 2048], BF16)
    ktbuf = sb("ktbuf", [128, 2 * 4096], BF16)
    ktcur = sb("ktcur", [128, 4096], BF16)
    xnt = sb("xnt", [128, 4096], BF16)
    qt = sb("qt", [128, 4096], BF16)
    xld = sb("xld", [128, 4096], F32)
    xnbf = sb("xnbf", [128, 2048], BF16)
    xbfres = sb("xbfres", [128, 4096], BF16)
    outst = sb("outst", [128, 2048], F32)
    hT = sb("hT", [128, 512], BF16)
    dump = sb("dump", [128, 2048], BF16)
    sts = sb("sts", [128, 56], F32)
    css = sb("css", [128, 80], F32)
    rr = sb("rr", [128, 8], F32)
    msk = sb("msk", [128, 128], BF16)
    bql = sb("bql", [128, 8], F32)
    bkl = sb("bkl", [128, 8], F32)
    b1l = sb("b1l", [128, 32], F32)
    bvb2 = sb("bvb2", [1, 2048], BF16)
    onesbf = sb("ones_sb", [1, 128], BF16)
    onesf = sb("onesf", [1, 128], F32)
    onescol = sb("onescol", [128, 1], F32)
    ident = sb("ident_sb", [128, 128], BF16)
    trilm = sb("trilm_sb", [128, 128], BF16)
    acc1 = sb("acc1", [128, 16], F32)
    acc2 = acc1

    thrbc = sts[:, 48:52]
    thrpack = sts[0:1, 52:56]

    ps_qk = psum("ps_qk", [128, 1024], F32)
    ps_s = psum("ps_s", [128, 1024], F32)
    ps_o2a = psum("ps_o2a", [128, 1024], F32)
    ps_o2b = psum("ps_o2b", [128, 1024], F32)
    ps_o2 = (ps_o2a, ps_o2b)
    psqk_bf = ps_qk[:].bitcast(BF16)
    pss_bf = ps_s[:].bitcast(BF16)

    sV = em.sem("sV")
    sA = em.sem("sA")
    sP = em.sem("sP")
    sL = em.sem("sL")
    s_x = [em.sem("s_x0"), em.sem("s_x1")]
    s_kt = [em.sem("s_kt0"), em.sem("s_kt1")]
    s_w2 = [em.sem("s_w20"), em.sem("s_w21")]
    s_pl = [em.sem(f"s_pl{i}") for i in range(4)]
    s_tp = [em.sem("s_tp0"), em.sem("s_tp1")]
    s_ktw = em.sem("s_ktw")
    s_ow = [em.sem("s_ow0"), em.sem("s_ow1")]
    s_gp = [em.sem(f"s_gp{i}") for i in range(4)]
    s_wl = em.sem("s_wl")
    em.esem = {"vector": sV, "scalar": sA, "tensor": sP}

    E = em.op
    DMA = em.dma

    _slots = {}
    _next = [0]

    def slot(name):
        if name not in _slots:
            _slots[name] = _next[0]
            _next[0] += 1
            assert _next[0] <= 48
        c = _slots[name]
        return sts[:, c:c + 1], f"sts:{name}"

    DMA("sync", sL, bql[:], bq_h.ap(), writes=["bql"])
    DMA("sync", sL, bkl[:], bk_h.ap(), writes=["bkl"])
    DMA("sync", sL, b1l[:], b1_h.ap(), writes=["b1l"])
    DMA("sync", sL, ident[:], id_h.ap(), writes=["ident"])
    DMA("sync", sL, trilm[:], tril_h.ap(), writes=["trilm"])
    DMA("sync", sL, onesbf[:], ones_h.ap(), writes=["onesbf"])
    DMA("sync", sL, xld[0:1, 0:1024], bv_h.ap(), writes=["xld00"])
    DMA("sync", sL, xld[0:1, 1024:2048], b2_h.ap(), writes=["xld01"])
    E("vector", "tensor_copy", (bvb2[0:1, 0:1024], xld[0:1, 0:1024]), reads=["xld00"], writes=["bvb2"])
    E("vector", "tensor_copy", (bvb2[0:1, 1024:2048], xld[0:1, 1024:2048]), reads=["xld01"], writes=["bvb2"])
    E("vector", "memset", (onescol[:], 1.0), writes=["onescol"])
    E("vector", "memset", (onesf[:], 1.0), writes=["onesf"])

    for w_hh, w_ss, key in ((wq_h, wq_sb, "wq"), (wk_h, wk_sb, "wk"), (wv_h, wv_sb, "wv")):
        DMA("gpsimd", s_wl,
            w_ss[:].rearrange("p (ki c) -> p ki c", ki=NJ),
            w_hh.ap().rearrange("(ki p) c -> p ki c", p=128),
            writes=[key])

    pbuf = (outst, None)
    pkeys = (["outst:0", "outst:1"], ["xld10", "xld11"])

    ktf32 = ktbuf[:].bitcast(F32)

    def emit_prep_p1w1():
        for tt in range(16):
            if tt % 2 == 0:
                buf, bkey = outst[:], ["outst:0", "outst:1"]
            else:
                buf, bkey = ktf32[:, 0:2048], ["ktbuf:0"]
            DMA("sync", s_pl[tt % 4],
                buf.rearrange("p (two c) -> p two c", two=2),
                w1_h.ap()[tt * 256:(tt + 1) * 256, :].rearrange("(two p) c -> p two c", p=128),
                writes=bkey)
            E("vector", "reduce_sum", (acc1[:, tt:tt + 1], buf),
              {"axis": AX.X, "apply_absolute_value": True},
              reads=bkey, writes=[f"acc:{tt}"])

    def emit_prep():
        def pB(par):
            return outst[:] if par == 0 else xld[:, 2048:4096]
        cs1, cs1k = slot("colsum1")
        E("vector", "reduce_sum", (cs1, acc1[:, 0:16]), {"axis": AX.X},
          reads=[f"acc:{t}" for t in range(16)], writes=[cs1k])
        E("tensor", "matmul", (ps_s[0:1, 0:1], cs1, onescol[:]), {"start": True, "stop": True},
          reads=[cs1k, "onescol"], writes=["pss:0"])
        NW = float(FF * D)
        E("vector", "tensor_scalar", (thrpack[0:1, 0:1], ps_s[0:1, 0:1], 0.5 / NW, 0.5e-8, ALU.mult, ALU.max),
          reads=["pss:0"], writes=["thrpack"])
        E("vector", "tensor_scalar", (thrpack[0:1, 1:2], thrpack[0:1, 0:1], -1.0, None, ALU.mult),
          reads=["thrpack"], writes=["thrpack"])
        E("tensor", "matmul", (ps_s[:, 8:10], onesf[:], thrpack[0:1, 0:2]), {"start": True, "stop": True},
          reads=["thrpack", "onesf"], writes=["pss:0"])
        E("vector", "tensor_copy", (thrbc[:, 0:2], ps_s[:, 8:10]), reads=["pss:0"], writes=["thrbc"])
        t1v, t1k = w2buf[:, 0:2048], ["w2buf:0"]
        t2v, t2k = w2buf[:, 2048:4096], ["w2buf:1"]

        def tern_tile(buf, bkey, thr_c, nthr_c, par):
            if par == 0:
                wtv, wtk = w2buf[:, 0:2048], ["w2buf:0"]
            else:
                wtv, wtk = dump[:, 0:2048], ["dump:0", "dump:1"]
            E("vector", "tensor_scalar", (t2v, buf, thrbc[:, nthr_c:nthr_c + 1], None, ALU.is_lt),
              reads=[bkey, "thrbc"], writes=t2k)
            E("vector", "scalar_tensor_tensor",
              (wtv, buf, thrbc[:, thr_c:thr_c + 1], t2v, ALU.is_gt, ALU.subtract),
              reads=[bkey, "thrbc"] + t2k, writes=wtk)
            return wtv, wtk

        for tt in range(16):
            buf, bkey = pB(tt % 2), pkeys[tt % 2][0:2]
            DMA("sync", s_pl[tt % 4],
                buf.rearrange("p (two c) -> p two c", two=2),
                w1_h.ap()[tt * 256:(tt + 1) * 256, :].rearrange("(two p) c -> p two c", p=128),
                writes=bkey)
            wtv, wtk = tern_tile(buf, bkey[0], 0, 1, tt % 2)
            tbf = psqk_bf if tt % 2 == 0 else pss_bf
            tkeys = ["psqk:0", "psqk:1"] if tt % 2 == 0 else ["pss:0", "pss:1"]
            grp = []
            for u in range(2):
                for j in range(NJ):
                    k = u * NJ + j
                    grp.append(("transpose",
                                (tbf[:, k * 128:(k + 1) * 128],
                                 wtv[:, u * 1024 + j * 128:u * 1024 + (j + 1) * 128],
                                 ident[:]), {}))
            E("tensor", None, None, group=grp, reads=wtk + ["ident"], writes=tkeys)
            for u in range(2):
                E("scalar", "copy",
                  (w1t_sb[:].rearrange("p (j f) -> p j f", j=NJ)[:, :, (tt * 2 + u) * 128:(tt * 2 + u + 1) * 128],
                   tbf[:, u * 1024:(u + 1) * 1024].rearrange("p (j c) -> p j c", j=NJ)),
                  reads=tkeys, writes=["w1t"])
            j2, qq2p = divmod(tt, 2)
            buf2 = ktf32[:, 2048:4096]
            bkey2 = ["ktbuf:1"]
            DMA("gpsimd", s_gp[tt % 4], buf2,
                w2_h.ap()[j2 * 128:(j2 + 1) * 128, qq2p * 2048:(qq2p + 1) * 2048],
                writes=bkey2)
            E("scalar", "activation", (dump[:, 0:2048], buf2, AF.Abs),
              {"accum_out": acc2[:, tt:tt + 1]},
              reads=bkey2, writes=["dump:0", "dump:1", f"acc:{tt}"])

        cs2, cs2k = slot("colsum2")
        E("vector", "reduce_sum", (cs2, acc2[:, 0:16]), {"axis": AX.X},
          reads=[f"acc:{t}" for t in range(16)], writes=[cs2k])
        E("tensor", "matmul", (ps_s[0:1, 4:5], cs2, onescol[:]), {"start": True, "stop": True},
          reads=[cs2k, "onescol"], writes=["pss:0"])
        NW2 = float(FF * D)
        E("vector", "tensor_scalar", (thrpack[0:1, 2:3], ps_s[0:1, 4:5], 0.5 / NW2, 0.5e-8, ALU.mult, ALU.max),
          reads=["pss:0"], writes=["thrpack"])
        E("vector", "tensor_scalar", (thrpack[0:1, 3:4], thrpack[0:1, 2:3], -1.0, None, ALU.mult),
          reads=["thrpack"], writes=["thrpack"])
        E("tensor", "matmul", (ps_s[:, 10:12], onesf[:], thrpack[0:1, 2:4]), {"start": True, "stop": True},
          reads=["thrpack", "onesf"], writes=["pss:0"])
        E("vector", "tensor_copy", (thrbc[:, 2:4], ps_s[:, 10:12]), reads=["pss:0"], writes=["thrbc"])
        for qq2 in range(2):
            for j in range(NJ):
                tt = qq2 * 8 + j
                buf, bkey = pB(tt % 2), pkeys[tt % 2][0:2]
                DMA("sync", s_pl[tt % 4], buf,
                    w2_h.ap()[j * 128:(j + 1) * 128, qq2 * 2048:(qq2 + 1) * 2048],
                    writes=bkey)
                wtv, wtk = tern_tile(buf, bkey[0], 2, 3, tt % 2)
                tbf = psqk_bf if tt % 2 == 0 else pss_bf
                tkeys = ["psqk:0", "psqk:1"] if tt % 2 == 0 else ["pss:0", "pss:1"]
                grp = []
                for fs in range(16):
                    grp.append(("transpose",
                                (tbf[:, fs * 128:(fs + 1) * 128],
                                 wtv[:, fs * 128:(fs + 1) * 128],
                                 ident[:]), {}))
                E("tensor", None, None, group=grp, reads=wtk + ["ident"], writes=tkeys)
                stage = ktbuf[:, (j % 2) * 4096:(j % 2) * 4096 + 2048]
                skey = f"ktbuf:{j % 2}"
                E("scalar", "copy", (stage, tbf[:, 0:2048]),
                  reads=tkeys, writes=[skey])
                DMA("gpsimd", s_gp[tt % 4],
                    w2t_h.ap()[qq2 * 2048:(qq2 + 1) * 2048, j * 128:(j + 1) * 128].rearrange(
                        "(fs p) c -> p fs c", p=128),
                    stage.rearrange("p (fs c) -> p fs c", fs=16),
                    reads=[skey], writes=[f"w2t:{qq2}:{j}"])

    NSUP = B * NST

    _dump_ctr = [0]

    def em_dump_next():
        _dump_ctr[0] ^= 1
        return _dump_ctr[0]

    def emit_x_loads(sup):
        b, g = divmod(sup, NST)
        p = sup % 2
        for i in range(2):
            r0 = b * SEQ + g * 256 + i * 128
            buf = xld[:, p * 2048 + i * 1024:p * 2048 + (i + 1) * 1024]
            DMA("sync", s_x[i], buf, x_h.ap()[r0:r0 + 128, :], writes=[f"xld{p}{i}"])

    def emit_ln_stats(sup):
        p = sup % 2
        bufs, bkeys, sl = [], [], []
        for i in range(2):
            bufs.append(xld[:, p * 2048 + i * 1024:p * 2048 + (i + 1) * 1024])
            bkeys.append(f"xld{p}{i}")
            sl.append({nm: slot(f"{nm}{i}_{p}") for nm in
                       ("mean0", "mean", "sumsq", "vp0", "msq", "vpe", "lnv", "rstd")})
        for i in range(2):
            E("vector", "reduce_sum", (sl[i]["mean0"][0], bufs[i]), {"axis": AX.X},
              reads=[bkeys[i]], writes=[sl[i]["mean0"][1]])
        for i in range(2):
            dpar = em_dump_next()
            E("scalar", "activation", (dump[:, dpar * 1024:dpar * 1024 + 1024], bufs[i], AF.Square),
              {"accum_out": sl[i]["sumsq"][0]},
              reads=[bkeys[i]], writes=[f"dump:{dpar}", sl[i]["sumsq"][1]])
        for i in range(2):
            E("vector", "tensor_scalar", (sl[i]["mean"][0], sl[i]["mean0"][0], 1.0 / D, None, ALU.mult),
              reads=[sl[i]["mean0"][1]], writes=[sl[i]["mean"][1]])
        for i in range(2):
            E("vector", "tensor_scalar", (sl[i]["vp0"][0], sl[i]["sumsq"][0], 1.0 / D, 1e-5, ALU.mult, ALU.add),
              reads=[sl[i]["sumsq"][1]], writes=[sl[i]["vp0"][1]])
        for i in range(2):
            E("vector", "tensor_tensor", (sl[i]["msq"][0], sl[i]["mean"][0], sl[i]["mean"][0], ALU.mult),
              reads=[sl[i]["mean"][1]], writes=[sl[i]["msq"][1]])
        for i in range(2):
            E("vector", "tensor_tensor", (sl[i]["vpe"][0], sl[i]["vp0"][0], sl[i]["msq"][0], ALU.subtract),
              reads=[sl[i]["vp0"][1], sl[i]["msq"][1]], writes=[sl[i]["vpe"][1]])
        for i in range(2):
            E("scalar", "activation", (sl[i]["lnv"][0], sl[i]["vpe"][0], AF.Ln),
              reads=[sl[i]["vpe"][1]], writes=[sl[i]["lnv"][1]])
        for i in range(2):
            E("scalar", "activation", (sl[i]["rstd"][0], sl[i]["lnv"][0], AF.Exp), {"scale": -0.5},
              reads=[sl[i]["lnv"][1]], writes=[sl[i]["rstd"][1]])

    def emit_ln_finish(sup):
        p = sup % 2
        for i in range(2):
            buf = xld[:, p * 2048 + i * 1024:p * 2048 + (i + 1) * 1024]
            bkey = f"xld{p}{i}"
            mean, meank = slot(f"mean{i}_{p}")
            rstd, rstdk = slot(f"rstd{i}_{p}")
            E("vector", "tensor_scalar",
              (xnbf[:, i * 1024:(i + 1) * 1024], buf, mean, rstd, ALU.subtract, ALU.mult),
              reads=[bkey, meank, rstdk], writes=[f"xnbf:{i}"])
            E("vector", "tensor_copy",
              (xbfres[:, p * 2048 + i * 1024:p * 2048 + (i + 1) * 1024], buf),
              reads=[bkey], writes=[f"xbfres:{p}:{i}"])

    attn_terms = {}

    def emit_rfinal(u):
        NPB = NST // 2
        b, gp = divmod(u, NPB)
        pp = u % 2
        for qp in range(4):
            n = attn_terms[u][qp]
            t_loc = 4 * gp + qp
            cnt = float(SEQ - 128 * (t_loc + 1))
            zs, zsk = slot(f"zs_{qp}")
            zf, zfk = slot(f"zf_{qp}")
            rz, rzk = slot(f"rz_{qp}")
            ckeys = [f"css:{pp}:{qp}:{i}" for i in range(n)]
            E("vector", "reduce_sum", (zs, css[:, pp * 40 + qp * 10:pp * 40 + qp * 10 + n]), {"axis": AX.X},
              reads=ckeys, writes=[zsk])
            E("vector", "tensor_scalar", (zf, zs, cnt, None, ALU.add),
              reads=[zsk], writes=[zfk])
            E("vector", "reciprocal", (rz, zf), reads=[zfk], writes=[rzk])
            E("vector", "tensor_tensor", (rr[:, pp * 4 + qp:pp * 4 + qp + 1], zf, rz, ALU.mult),
              reads=[zfk, rzk], writes=[f"rr:{pp}:{qp}"])

    def emit_attn_pair(u):
        NPB = NST // 2
        b, gp = divmod(u, NPB)
        pp = u % 2
        cbase = pp * 40

        for half in range(2):
            sup = 2 * u + half
            if half == 1:
                emit_ln_finish(sup)
            p = sup % 2
            xntv = xnt[:, p * 2048:(p + 1) * 2048]
            for i in range(2):
                grp = []
                for j in range(NJ):
                    grp.append(("transpose",
                                (psqk_bf[:, i * 1024 + j * 128:i * 1024 + (j + 1) * 128],
                                 xnbf[:, i * 1024 + j * 128:i * 1024 + (j + 1) * 128],
                                 ident[:]), {}))
                E("tensor", None, None, group=grp,
                  reads=[f"xnbf:{i}", "ident"], writes=[f"psqk:{i}"])
                E("vector", "tensor_copy",
                  (xntv.rearrange("p (j c) -> p j c", j=NJ)[:, :, i * 128:(i + 1) * 128],
                   psqk_bf[:, i * 1024:(i + 1) * 1024].rearrange("p (j c) -> p j c", j=NJ)),
                  reads=[f"psqk:{i}"], writes=[f"xnt:{p}"])

        for wsb, wkey, bias_sb, bias_key, dest, destkey in (
                (wq_sb, "wq", bql, "bql", qt, "qt"),
                (wk_sb, "wk", bkl, "bkl", ktcur, "ktcur")):
            for mj in range(NJ):
                bank = mj % 2
                grp = []
                for half in range(2):
                    xntv = xnt[:, (2 * u + half) % 2 * 2048:((2 * u + half) % 2 + 1) * 2048]
                    ps = ps_qk[:, bank * 512 + half * 256:bank * 512 + (half + 1) * 256]
                    for ki in range(NJ):
                        grp.append(("matmul",
                                    (ps, wsb[:, ki * D + mj * 128:ki * D + (mj + 1) * 128],
                                     xntv[:, ki * 256:(ki + 1) * 256]),
                                    {"start": half == 0 and ki == 0,
                                     "stop": half == 1 and ki == NJ - 1}))
                E("tensor", None, None, group=grp,
                  reads=[wkey, "xnt:0", "xnt:1"], writes=[f"psqk:{bank}"])
                E("scalar", "activation",
                  (dest[:, mj * 512:(mj + 1) * 512], ps_qk[:, bank * 512:(bank + 1) * 512],
                   AF.Identity),
                  {"bias": bias_sb[:, mj:mj + 1]},
                  reads=[f"psqk:{bank}", bias_key], writes=[destkey])

        for half in range(2):
            base = (b * NST + 2 * gp + half) * 8 * 128
            DMA("gpsimd", s_ktw,
                kthbm.ap()[base:base + 1024, :].rearrange("(j p) c -> p j c", p=128),
                ktcur[:].rearrange("p (j c) -> p j c", j=NJ)[:, :, half * 256:(half + 1) * 256],
                reads=["ktcur"], writes=[f"kthbm:{b}:{2 * gp + half}"])

        nch = gp

        def load_ktchunk(c):
            cb = c % 2
            for gg in range(2):
                kb = (b * NST + 2 * c + gg) * 8 * 128
                DMA("sync", s_kt[cb],
                    ktbuf[:, cb * 4096:(cb + 1) * 4096].rearrange(
                        "p (j gg c) -> p gg j c", j=NJ, gg=2)[:, gg],
                    kthbm.ap()[kb:kb + 1024, :].rearrange("(j p) c -> p j c", p=128),
                    reads=[f"kthbm:{b}:{2 * c + gg}"],
                    writes=[f"ktbuf:{cb}"])

        for c in range(min(nch, 2)):
            load_ktchunk(c)

        state = [{"n": 0} for _ in range(4)]

        def term(qp, src_ap, src_keys, width):
            st = state[qp]
            idx = st["n"]
            st["n"] += 1
            cs = css[:, cbase + qp * 10 + idx:cbase + qp * 10 + idx + 1]
            csk = f"css:{pp}:{qp}:{idx}"
            dpar = em_dump_next()
            E("scalar", "activation",
              (dump[:, dpar * 1024:dpar * 1024 + width], src_ap, AF.Exp),
              {"scale": 1.0 / 32.0, "accum_out": cs},
              reads=src_keys, writes=[f"dump:{dpar}", csk])

        SB4 = ((ps_s, 0, "pss:0"), (ps_s, 512, "pss:1"),
               (ps_qk, 0, "psqk:0"), (ps_qk, 512, "psqk:1"))

        for c in range(nch):
            cb = c % 2
            for qp in range(4):
                pst, off, bkey_ps = SB4[qp]
                ps = pst[:, off:off + 512]
                grp = []
                for j in range(NJ):
                    grp.append(("matmul",
                                (ps, qt[:, j * 512 + qp * 128:j * 512 + (qp + 1) * 128],
                                 ktbuf[:, cb * 4096 + j * 512:cb * 4096 + (j + 1) * 512]),
                                {"start": j == 0, "stop": j == NJ - 1}))
                E("tensor", None, None, group=grp,
                  reads=["qt", f"ktbuf:{cb}"], writes=[bkey_ps])
                term(qp, ps, [bkey_ps], 512)
            if c + 2 < nch:
                load_ktchunk(c + 2)
        for qp in range(1, 4):
            rem = qp * 128
            start = 0
            while rem > 0:
                w = min(256, rem)
                pst, off, bkey_ps = SB4[qp]
                ps = pst[:, off:off + w]
                grp = []
                for j in range(NJ):
                    grp.append(("matmul",
                                (ps, qt[:, j * 512 + qp * 128:j * 512 + (qp + 1) * 128],
                                 ktcur[:, j * 512 + start:j * 512 + start + w]),
                                {"start": j == 0, "stop": j == NJ - 1}))
                E("tensor", None, None, group=grp, reads=["qt", "ktcur"], writes=[bkey_ps])
                term(qp, ps, [bkey_ps], w)
                start += w
                rem -= w
        for qp in range(4):
            pst, off, bkey_ps = SB4[qp]
            ps = pst[:, off:off + 128]
            grp = []
            for j in range(NJ):
                grp.append(("matmul",
                            (ps, qt[:, j * 512 + qp * 128:j * 512 + (qp + 1) * 128],
                             ktcur[:, j * 512 + qp * 128:j * 512 + (qp + 1) * 128]),
                            {"start": j == 0, "stop": j == NJ - 1}))
            E("tensor", None, None, group=grp, reads=["qt", "ktcur"], writes=[bkey_ps])
            mv = msk[:, 0:128]
            E("vector", "tensor_tensor", (mv, ps, trilm[:], ALU.mult),
              reads=[bkey_ps, "trilm"], writes=["msk"])
            term(qp, mv, ["msk"], 128)
        attn_terms[u] = tuple(state[qp]["n"] for qp in range(4))
        emit_rfinal(u)

    def emit_ffn(sup):
        b, g = divmod(sup, NST)
        p = sup % 2
        pp = (sup // 2) % 2
        xntv = xnt[:, p * 2048:(p + 1) * 2048]
        xntk = f"xnt:{p}"

        def load_w2chunk(k):
            wb = k % 2
            DMA("sync", s_w2[wb],
                w2buf[:, wb * 2048:(wb + 1) * 2048].rearrange("p (ft c) -> p ft c", ft=2),
                w2t_h.ap()[k * 256:(k + 1) * 256, :].rearrange("(ft p) c -> p ft c", p=128),
                reads=[f"w2t:{k // 8}:{j}" for j in range(NJ)], writes=[f"w2buf:{wb}"])

        load_w2chunk(0)
        load_w2chunk(1)
        def emit_h(f):
            bank = f % 2
            ps = ps_qk[:, bank * 512:bank * 512 + 256]
            grp = []
            for ki in range(NJ):
                grp.append(("matmul",
                            (ps, w1t_sb[:, ki * FF + f * 128:ki * FF + (f + 1) * 128],
                             xntv[:, ki * 256:(ki + 1) * 256]),
                            {"start": ki == 0, "stop": ki == NJ - 1}))
            E("tensor", None, None, group=grp, reads=["w1t", xntk], writes=[f"psqk:{bank}"])
            E("scalar", "activation",
              (hT[:, bank * 256:(bank + 1) * 256], ps, AF.Gelu),
              {"bias": b1l[:, f:f + 1]},
              reads=[f"psqk:{bank}", "b1l"], writes=[f"hT:{bank}"])

        def emit_o2(f):
            bank = f % 2
            wb = (f // 2) % 2
            grp = []
            reads = [f"hT:{bank}", f"w2buf:{wb}"]
            if f == 0:
                for s in range(2):
                    for half in range(2):
                        grp.append(("matmul",
                                    (ps_o2[s][:, half * 512:(half + 1) * 512],
                                     onesbf[:],
                                     bvb2[0:1, 1024 + half * 512:1024 + (half + 1) * 512]),
                                    {"start": True, "stop": False}))
                reads += ["onesbf", "bvb2"]
            for s in range(2):
                for half in range(2):
                    grp.append(("matmul",
                                (ps_o2[s][:, half * 512:(half + 1) * 512],
                                 hT[:, bank * 256 + s * 128:bank * 256 + (s + 1) * 128],
                                 w2buf[:, wb * 2048 + (f % 2) * 1024 + half * 512:
                                       wb * 2048 + (f % 2) * 1024 + (half + 1) * 512]),
                                {"start": False, "stop": f == NFT - 1}))
            E("tensor", None, None, group=grp, reads=reads,
              writes=(["pso2a", "pso2b"] if (f == 0 or f == NFT - 1) else []),
              extra=([] if f == 0 else [em.W["pso2a"], em.W["pso2b"]]))

        def emit_v(s):
            grp = []
            for half in range(2):
                ps = ps_s[:, half * 512:(half + 1) * 512]
                grp.append(("matmul",
                            (ps, onesbf[:], bvb2[0:1, half * 512:(half + 1) * 512]),
                            {"start": True, "stop": False}))
                for ki in range(NJ):
                    grp.append(("matmul",
                                (ps, xntv[:, ki * 256 + s * 128:ki * 256 + (s + 1) * 128],
                                 wv_sb[:, ki * D + half * 512:ki * D + (half + 1) * 512]),
                                {"start": False, "stop": ki == NJ - 1}))
            E("tensor", None, None, group=grp,
              reads=["onesbf", "bvb2", xntk, "wv"], writes=["pss:0", "pss:1"])
            qp = (sup % 2) * 2 + s
            E("vector", "scalar_tensor_tensor",
              (outst[:, s * 1024:(s + 1) * 1024], ps_s[:, 0:1024], rr[:, pp * 4 + qp:pp * 4 + qp + 1],
               xbfres[:, p * 2048 + s * 1024:p * 2048 + (s + 1) * 1024],
               ALU.mult, ALU.add),
              reads=["pss:0", "pss:1", f"rr:{pp}:{qp}", f"xbfres:{p}:{s}"],
              writes=[f"outst:{s}"])

        emit_h(0)
        emit_h(1)
        emit_v(0)
        for f in range(2, NFT):
            emit_o2(f - 2)
            if (f - 2) % 2 == 1 and (f - 2) // 2 + 2 < 16:
                load_w2chunk((f - 2) // 2 + 2)
            if f == 4:
                emit_v(1)
            emit_h(f)
        emit_o2(NFT - 2)
        emit_o2(NFT - 1)
        if sup + 2 < NSUP:
            emit_ln_stats(sup + 2)

        for s in range(2):
            E("vector", "scalar_tensor_tensor",
              (outst[:, s * 1024:(s + 1) * 1024], ps_o2[s][:, 0:1024], 0.0,
               outst[:, s * 1024:(s + 1) * 1024], ALU.bypass, ALU.add),
              reads=[("pso2a" if s == 0 else "pso2b"), f"outst:{s}"],
              writes=[f"outst:{s}"])
            r0 = b * SEQ + g * 256 + s * 128
            DMA("gpsimd", s_ow[s], out_h.ap()[r0:r0 + 128, :], outst[:, s * 1024:(s + 1) * 1024],
                reads=[f"outst:{s}"], writes=[])

    NP = NSUP // 2
    emit_x_loads(0)
    emit_ln_stats(0)
    emit_x_loads(1)
    emit_ln_stats(1)
    emit_ln_finish(0)
    emit_prep_p1w1()
    emit_attn_pair(0)
    if NSUP > 2:
        emit_x_loads(2)
    emit_prep()
    for u in range(1, NP):
        emit_ffn(2 * u - 2)
        if 2 * u + 1 < NSUP:
            emit_x_loads(2 * u + 1)
        emit_ln_finish(2 * u)
        emit_ffn(2 * u - 1)
        if 2 * u + 2 < NSUP:
            emit_x_loads(2 * u + 2)
        emit_attn_pair(u)
    emit_ffn(NSUP - 2)
    emit_ffn(NSUP - 1)

    em.ops["gpsimd"].append(([(s_ow[0], s_ow[0].n), (s_ow[1], s_ow[1].n), (s_ktw, s_ktw.n)],
                             [], (None, 0)))

    for s in em.sems:
        s.h = ctx.enter_context(nc.semaphore(s.name))
    blk = ctx.enter_context(nc.Block())

    def runner(lst):
        def go(eng):
            for waits, group, (s, amt) in lst:
                for ws, wv in waits:
                    eng.wait_ge(ws.h, wv)
                last = None
                for method, args, kwargs in group:
                    last = getattr(eng, method)(*args, **kwargs)
                if last is not None and s is not None:
                    last.then_inc(s.h, amt)
        return go

    blk.sync(runner(em.ops["sync"]))
    blk.gpsimd(runner(em.ops["gpsimd"]))
    blk.vector(runner(em.ops["vector"]))
    blk.scalar(runner(em.ops["scalar"]))
    blk.tensor(runner(em.ops["tensor"]))
    ctx.close()
    return nc


def _host_inputs(L0_attn_Wq, L0_attn_bq, L0_attn_Wk, L0_attn_bk, L0_attn_Wv, L0_attn_bv,
                 L0_ffn_W1, L0_ffn_b1, L0_ffn_W2, L0_ffn_b2):
    f32 = lambda a: np.ascontiguousarray(np.asarray(a, dtype=np.float32))
    common = {
        "wq": f32(L0_attn_Wq), "wk": f32(L0_attn_Wk), "wv": f32(L0_attn_Wv),
        "w1": f32(L0_ffn_W1), "w2": f32(L0_ffn_W2),
        "bq_l": f32(np.asarray(L0_attn_bq).reshape(8, 128).T),
        "bk_l": f32(np.asarray(L0_attn_bk).reshape(8, 128).T),
        "b1_l": f32(np.asarray(L0_ffn_b1).reshape(32, 128).T),
        "bv_r": f32(np.asarray(L0_attn_bv).reshape(1, D)),
        "b2_r": f32(np.asarray(L0_ffn_b2).reshape(1, D)),
        "ident": np.eye(128, dtype=ml_dtypes.bfloat16),
        "trilm": np.tril(np.ones((128, 128), dtype=ml_dtypes.bfloat16)),
        "onesbf": np.ones((1, 128), dtype=ml_dtypes.bfloat16),
    }
    return common


def kernel(L0_attn_X, L0_attn_Wq, L0_attn_bq, L0_attn_Wk, L0_attn_bk,
           L0_attn_Wv, L0_attn_bv, L0_ffn_W1, L0_ffn_b1, L0_ffn_W2, L0_ffn_b2):
    NB, SEQ, _ = L0_attn_X.shape
    ncores = 8
    bpc = NB // ncores
    X = np.ascontiguousarray(np.asarray(L0_attn_X, dtype=np.float32)).reshape(NB * SEQ, D)
    common = _host_inputs(L0_attn_Wq, L0_attn_bq, L0_attn_Wk, L0_attn_bk, L0_attn_Wv,
                          L0_attn_bv, L0_ffn_W1, L0_ffn_b1, L0_ffn_W2, L0_ffn_b2)
    nc = bass.Bass("TRN2", target_bir_lowering=False)
    build_program(nc, bpc, SEQ)
    in_maps = []
    for c in range(ncores):
        m = dict(common)
        m["x"] = np.ascontiguousarray(X[c * bpc * SEQ:(c + 1) * bpc * SEQ, :])
        in_maps.append(m)
    res = run_bass_kernel_spmd(nc, in_maps, core_ids=list(range(ncores)))
    out = np.concatenate([np.asarray(r["out"]) for r in res.results], axis=0)
    return out.reshape(NB, SEQ, D)


# version 24
# speedup vs baseline: 1.0945x; 1.0069x over previous
import numpy as np
import ml_dtypes
from contextlib import ExitStack

import concourse.bass as bass
import concourse.mybir as mybir
from concourse.bass_utils import run_bass_kernel_spmd

F32 = mybir.dt.float32
BF16 = mybir.dt.bfloat16
AF = mybir.ActivationFunctionType
ALU = mybir.AluOpType
AX = mybir.AxisListType

D = 1024
FF = 4096
NJ = 8
NFT = 32
ENGS = ("sync", "gpsimd", "vector", "scalar", "tensor")


class Sem:
    __slots__ = ("name", "h", "n")

    def __init__(self, name):
        self.name = name
        self.h = None
        self.n = 0


class Emitter:
    def __init__(self):
        self.ops = {e: [] for e in ENGS}
        self.sems = []
        self.esem = {}
        self.waited = {e: {} for e in ENGS}
        self.W = {}
        self.R = {}

    def sem(self, name):
        s = Sem(name)
        self.sems.append(s)
        return s

    def _waits(self, engine, reads, writes, extra):
        acc = {}

        def add(sv):
            if sv is None:
                return
            s, v = sv
            if v <= 0:
                return
            cur = acc.get(s.name)
            if cur is None or cur[1] < v:
                acc[s.name] = (s, v)

        for k in reads:
            add(self.W.get(k))
        for k in writes:
            add(self.W.get(k))
            for sv in self.R.get(k, {}).values():
                add(sv)
        for sv in extra:
            add(sv)
        out = []
        for s, v in acc.values():
            assert v <= s.n, f"wait on future value of {s.name}: {v} > {s.n}"
            if self.waited[engine].get(s.name, 0) >= v:
                continue
            self.waited[engine][s.name] = v
            out.append((s, v))
        return out

    def op(self, engine, method, args, kwargs=None, reads=(), writes=(), extra=(),
           sem=None, amt=1, group=None):
        waits = self._waits(engine, reads, writes, extra)
        s = sem if sem is not None else self.esem[engine]
        s.n += amt
        v = s.n
        if group is None:
            group = [(method, args, kwargs or {})]
        self.ops[engine].append((waits, group, (s, amt)))
        for k in writes:
            self.W[k] = (s, v)
            self.R[k] = {}
        for k in reads:
            d = self.R.setdefault(k, {})
            cur = d.get(s.name)
            if cur is None or cur[1] < v:
                d[s.name] = (s, v)
        return (s, v)

    def dma(self, qeng, qsem, dst, src, reads=(), writes=(), extra=(), serialize=True, transpose=False):
        ex = list(extra)
        if qsem.n > 0:
            ex.append((qsem, qsem.n))
        method = "dma_start_transpose" if transpose else "dma_start"
        return self.op(qeng, method, (dst, src), None, reads, writes, ex, sem=qsem, amt=16)


def build_program(nc, B, SEQ):
    NST = SEQ // 256
    ROWS = B * SEQ
    em = Emitter()

    x_h = nc.dram_tensor("x", [ROWS, D], F32, kind="ExternalInput")
    wq_h = nc.dram_tensor("wq", [D, D], F32, kind="ExternalInput")
    wk_h = nc.dram_tensor("wk", [D, D], F32, kind="ExternalInput")
    wv_h = nc.dram_tensor("wv", [D, D], F32, kind="ExternalInput")
    w1_h = nc.dram_tensor("w1", [FF, D], F32, kind="ExternalInput")
    w2_h = nc.dram_tensor("w2", [D, FF], F32, kind="ExternalInput")
    bq_h = nc.dram_tensor("bq_l", [128, 8], F32, kind="ExternalInput")
    bk_h = nc.dram_tensor("bk_l", [128, 8], F32, kind="ExternalInput")
    b1_h = nc.dram_tensor("b1_l", [128, 32], F32, kind="ExternalInput")
    bv_h = nc.dram_tensor("bv_r", [1, D], F32, kind="ExternalInput")
    b2_h = nc.dram_tensor("b2_r", [1, D], F32, kind="ExternalInput")
    id_h = nc.dram_tensor("ident", [128, 128], BF16, kind="ExternalInput")
    tril_h = nc.dram_tensor("trilm", [128, 128], BF16, kind="ExternalInput")
    ones_h = nc.dram_tensor("onesbf", [1, 128], BF16, kind="ExternalInput")
    out_h = nc.dram_tensor("out", [ROWS, D], F32, kind="ExternalOutput")
    kthbm = nc.dram_tensor("kthbm", [B * NST * 8 * 128, 256], BF16, kind="Internal")
    w1tern_h = nc.dram_tensor("w1tern", [FF, D], BF16, kind="Internal")
    w2tern_h = nc.dram_tensor("w2tern", [D, FF], BF16, kind="Internal")
    w2t_h = nc.dram_tensor("w2t", [FF, D], BF16, kind="Internal")

    ctx = ExitStack()

    def sb(name, shape, dt):
        return ctx.enter_context(nc.sbuf_tensor(name, shape, dt))

    def psum(name, shape, dt):
        return ctx.enter_context(nc.psum_tensor(name, shape, dt))

    wq_sb = sb("wq_sb", [128, NJ * D], BF16)
    wk_sb = sb("wk_sb", [128, NJ * D], BF16)
    wv_sb = sb("wv_sb", [128, NJ * D], BF16)
    w1t_sb = sb("w1t_sb", [128, NJ * FF], BF16)
    w2buf = sb("w2buf", [128, 2 * 2048], BF16)
    ktbuf = sb("ktbuf", [128, 2 * 4096], BF16)
    ktcur = sb("ktcur", [128, 4096], BF16)
    xnt = sb("xnt", [128, 4096], BF16)
    qt = sb("qt", [128, 4096], BF16)
    xld = sb("xld", [128, 4096], F32)
    xnbf = sb("xnbf", [128, 2048], BF16)
    xbfres = sb("xbfres", [128, 4096], BF16)
    outst = sb("outst", [128, 2048], F32)
    hT = sb("hT", [128, 512], BF16)
    dump = sb("dump", [128, 2048], BF16)
    sts = sb("sts", [128, 56], F32)
    css = sb("css", [128, 80], F32)
    rr = sb("rr", [128, 8], F32)
    msk = sb("msk", [128, 128], BF16)
    bql = sb("bql", [128, 8], F32)
    bkl = sb("bkl", [128, 8], F32)
    b1l = sb("b1l", [128, 32], F32)
    bvb2 = sb("bvb2", [1, 2048], BF16)
    onesbf = sb("ones_sb", [1, 128], BF16)
    onesf = sb("onesf", [1, 128], F32)
    onescol = sb("onescol", [128, 1], F32)
    ident = sb("ident_sb", [128, 128], BF16)
    trilm = sb("trilm_sb", [128, 128], BF16)
    acc1 = sb("acc1", [128, 16], F32)
    acc2 = acc1

    thrbc = sts[:, 48:52]
    thrpack = sts[0:1, 52:56]

    ps_qk = psum("ps_qk", [128, 1024], F32)
    ps_s = psum("ps_s", [128, 1024], F32)
    ps_o2a = psum("ps_o2a", [128, 1024], F32)
    ps_o2b = psum("ps_o2b", [128, 1024], F32)
    ps_o2 = (ps_o2a, ps_o2b)
    psqk_bf = ps_qk[:].bitcast(BF16)
    pss_bf = ps_s[:].bitcast(BF16)

    sV = em.sem("sV")
    sA = em.sem("sA")
    sP = em.sem("sP")
    sL = em.sem("sL")
    s_x = [em.sem("s_x0"), em.sem("s_x1")]
    s_kt = [em.sem("s_kt0"), em.sem("s_kt1")]
    s_w2 = [em.sem("s_w20"), em.sem("s_w21")]
    s_pl = [em.sem(f"s_pl{i}") for i in range(4)]
    s_tp = [em.sem("s_tp0"), em.sem("s_tp1")]
    s_ktw = em.sem("s_ktw")
    s_ow = [em.sem("s_ow0"), em.sem("s_ow1")]
    s_gp = [em.sem(f"s_gp{i}") for i in range(4)]
    s_wl = em.sem("s_wl")
    em.esem = {"vector": sV, "scalar": sA, "tensor": sP}

    E = em.op
    DMA = em.dma

    _slots = {}
    _next = [0]

    def slot(name):
        if name not in _slots:
            _slots[name] = _next[0]
            _next[0] += 1
            assert _next[0] <= 48
        c = _slots[name]
        return sts[:, c:c + 1], f"sts:{name}"

    DMA("sync", sL, bql[:], bq_h.ap(), writes=["bql"])
    DMA("sync", sL, bkl[:], bk_h.ap(), writes=["bkl"])
    DMA("sync", sL, b1l[:], b1_h.ap(), writes=["b1l"])
    DMA("sync", sL, ident[:], id_h.ap(), writes=["ident"])
    DMA("sync", sL, trilm[:], tril_h.ap(), writes=["trilm"])
    DMA("sync", sL, onesbf[:], ones_h.ap(), writes=["onesbf"])
    DMA("sync", sL, xld[0:1, 0:1024], bv_h.ap(), writes=["xld00"])
    DMA("sync", sL, xld[0:1, 1024:2048], b2_h.ap(), writes=["xld01"])
    E("vector", "tensor_copy", (bvb2[0:1, 0:1024], xld[0:1, 0:1024]), reads=["xld00"], writes=["bvb2"])
    E("vector", "tensor_copy", (bvb2[0:1, 1024:2048], xld[0:1, 1024:2048]), reads=["xld01"], writes=["bvb2"])
    E("vector", "memset", (onescol[:], 1.0), writes=["onescol"])
    E("vector", "memset", (onesf[:], 1.0), writes=["onesf"])

    for w_hh, w_ss, key in ((wq_h, wq_sb, "wq"), (wk_h, wk_sb, "wk"), (wv_h, wv_sb, "wv")):
        DMA("gpsimd", s_wl,
            w_ss[:].rearrange("p (ki c) -> p ki c", ki=NJ),
            w_hh.ap().rearrange("(ki p) c -> p ki c", p=128),
            writes=[key])

    pbuf = (outst, None)
    pkeys = (["outst:0", "outst:1"], ["xld10", "xld11"])

    def emit_prep():
        def pB(par):
            return outst[:] if par == 0 else xld[:, 2048:4096]
        for tt in range(16):
            buf, bkey = pB(tt % 2), pkeys[tt % 2]
            DMA("sync", s_pl[tt % 4],
                buf.rearrange("p (two c) -> p two c", two=2),
                w1_h.ap()[tt * 256:(tt + 1) * 256, :].rearrange("(two p) c -> p two c", p=128),
                writes=bkey)
            E("scalar", "activation", (dump[:, 0:2048], buf, AF.Abs),
              {"accum_out": acc1[:, tt:tt + 1]},
              reads=bkey, writes=["dump:0", "dump:1", f"acc:{tt}"])
        cs1, cs1k = slot("colsum1")
        E("vector", "reduce_sum", (cs1, acc1[:, 0:16]), {"axis": AX.X},
          reads=[f"acc:{t}" for t in range(16)], writes=[cs1k])
        E("tensor", "matmul", (ps_s[0:1, 0:1], cs1, onescol[:]), {"start": True, "stop": True},
          reads=[cs1k, "onescol"], writes=["pss:0"])
        NW = float(FF * D)
        E("vector", "tensor_scalar", (thrpack[0:1, 0:1], ps_s[0:1, 0:1], 0.5 / NW, 0.5e-8, ALU.mult, ALU.max),
          reads=["pss:0"], writes=["thrpack"])
        E("vector", "tensor_scalar", (thrpack[0:1, 1:2], thrpack[0:1, 0:1], -1.0, None, ALU.mult),
          reads=["thrpack"], writes=["thrpack"])
        E("tensor", "matmul", (ps_s[:, 8:10], onesf[:], thrpack[0:1, 0:2]), {"start": True, "stop": True},
          reads=["thrpack", "onesf"], writes=["pss:0"])
        E("vector", "tensor_copy", (thrbc[:, 0:2], ps_s[:, 8:10]), reads=["pss:0"], writes=["thrbc"])
        t1v, t1k = w2buf[:, 0:2048], ["w2buf:0"]
        t2v, t2k = w2buf[:, 2048:4096], ["w2buf:1"]

        def tern_tile(buf, bkey, thr_c, nthr_c, par):
            if par == 0:
                wtv, wtk = w2buf[:, 0:2048], ["w2buf:0"]
            else:
                wtv, wtk = dump[:, 0:2048], ["dump:0", "dump:1"]
            E("vector", "tensor_scalar", (t2v, buf, thrbc[:, nthr_c:nthr_c + 1], None, ALU.is_lt),
              reads=[bkey, "thrbc"], writes=t2k)
            E("vector", "scalar_tensor_tensor",
              (wtv, buf, thrbc[:, thr_c:thr_c + 1], t2v, ALU.is_gt, ALU.subtract),
              reads=[bkey, "thrbc"] + t2k, writes=wtk)
            return wtv, wtk

        for tt in range(16):
            buf, bkey = pB(tt % 2), pkeys[tt % 2][0:2]
            DMA("sync", s_pl[tt % 4],
                buf.rearrange("p (two c) -> p two c", two=2),
                w1_h.ap()[tt * 256:(tt + 1) * 256, :].rearrange("(two p) c -> p two c", p=128),
                writes=bkey)
            wtv, wtk = tern_tile(buf, bkey[0], 0, 1, tt % 2)
            tbf = psqk_bf if tt % 2 == 0 else pss_bf
            tkeys = ["psqk:0", "psqk:1"] if tt % 2 == 0 else ["pss:0", "pss:1"]
            grp = []
            for u in range(2):
                for j in range(NJ):
                    k = u * NJ + j
                    grp.append(("transpose",
                                (tbf[:, k * 128:(k + 1) * 128],
                                 wtv[:, u * 1024 + j * 128:u * 1024 + (j + 1) * 128],
                                 ident[:]), {}))
            E("tensor", None, None, group=grp, reads=wtk + ["ident"], writes=tkeys)
            for u in range(2):
                E("scalar", "copy",
                  (w1t_sb[:].rearrange("p (j f) -> p j f", j=NJ)[:, :, (tt * 2 + u) * 128:(tt * 2 + u + 1) * 128],
                   tbf[:, u * 1024:(u + 1) * 1024].rearrange("p (j c) -> p j c", j=NJ)),
                  reads=tkeys, writes=["w1t"])
            ktf32 = ktbuf[:].bitcast(F32)
            j2, qq2p = divmod(tt, 2)
            buf2 = ktf32[:, (tt % 2) * 2048:(tt % 2 + 1) * 2048]
            bkey2 = [f"ktbuf:{tt % 2}"]
            DMA("gpsimd", s_gp[tt % 4], buf2,
                w2_h.ap()[j2 * 128:(j2 + 1) * 128, qq2p * 2048:(qq2p + 1) * 2048],
                writes=bkey2)
            E("scalar", "activation", (dump[:, 0:2048], buf2, AF.Abs),
              {"accum_out": acc2[:, tt:tt + 1]},
              reads=bkey2, writes=["dump:0", "dump:1", f"acc:{tt}"])

        cs2, cs2k = slot("colsum2")
        E("vector", "reduce_sum", (cs2, acc2[:, 0:16]), {"axis": AX.X},
          reads=[f"acc:{t}" for t in range(16)], writes=[cs2k])
        E("tensor", "matmul", (ps_s[0:1, 4:5], cs2, onescol[:]), {"start": True, "stop": True},
          reads=[cs2k, "onescol"], writes=["pss:0"])
        NW2 = float(FF * D)
        E("vector", "tensor_scalar", (thrpack[0:1, 2:3], ps_s[0:1, 4:5], 0.5 / NW2, 0.5e-8, ALU.mult, ALU.max),
          reads=["pss:0"], writes=["thrpack"])
        E("vector", "tensor_scalar", (thrpack[0:1, 3:4], thrpack[0:1, 2:3], -1.0, None, ALU.mult),
          reads=["thrpack"], writes=["thrpack"])
        E("tensor", "matmul", (ps_s[:, 10:12], onesf[:], thrpack[0:1, 2:4]), {"start": True, "stop": True},
          reads=["thrpack", "onesf"], writes=["pss:0"])
        E("vector", "tensor_copy", (thrbc[:, 2:4], ps_s[:, 10:12]), reads=["pss:0"], writes=["thrbc"])
        for qq2 in range(2):
            for j in range(NJ):
                tt = qq2 * 8 + j
                buf, bkey = pB(tt % 2), pkeys[tt % 2][0:2]
                DMA("sync", s_pl[tt % 4], buf,
                    w2_h.ap()[j * 128:(j + 1) * 128, qq2 * 2048:(qq2 + 1) * 2048],
                    writes=bkey)
                wtv, wtk = tern_tile(buf, bkey[0], 2, 3, tt % 2)
                tbf = psqk_bf if tt % 2 == 0 else pss_bf
                tkeys = ["psqk:0", "psqk:1"] if tt % 2 == 0 else ["pss:0", "pss:1"]
                grp = []
                for fs in range(16):
                    grp.append(("transpose",
                                (tbf[:, fs * 128:(fs + 1) * 128],
                                 wtv[:, fs * 128:(fs + 1) * 128],
                                 ident[:]), {}))
                E("tensor", None, None, group=grp, reads=wtk + ["ident"], writes=tkeys)
                stage = ktbuf[:, (j % 2) * 4096:(j % 2) * 4096 + 2048]
                skey = f"ktbuf:{j % 2}"
                E("scalar", "copy", (stage, tbf[:, 0:2048]),
                  reads=tkeys, writes=[skey])
                DMA("gpsimd", s_gp[tt % 4],
                    w2t_h.ap()[qq2 * 2048:(qq2 + 1) * 2048, j * 128:(j + 1) * 128].rearrange(
                        "(fs p) c -> p fs c", p=128),
                    stage.rearrange("p (fs c) -> p fs c", fs=16),
                    reads=[skey], writes=[f"w2t:{qq2}:{j}"])

    NSUP = B * NST

    _dump_ctr = [0]

    def em_dump_next():
        _dump_ctr[0] ^= 1
        return _dump_ctr[0]

    def emit_x_loads(sup):
        b, g = divmod(sup, NST)
        p = sup % 2
        for i in range(2):
            r0 = b * SEQ + g * 256 + i * 128
            buf = xld[:, p * 2048 + i * 1024:p * 2048 + (i + 1) * 1024]
            DMA("sync", s_x[i], buf, x_h.ap()[r0:r0 + 128, :], writes=[f"xld{p}{i}"])

    def emit_ln_stats(sup):
        p = sup % 2
        bufs, bkeys, sl = [], [], []
        for i in range(2):
            bufs.append(xld[:, p * 2048 + i * 1024:p * 2048 + (i + 1) * 1024])
            bkeys.append(f"xld{p}{i}")
            sl.append({nm: slot(f"{nm}{i}_{p}") for nm in
                       ("mean0", "mean", "sumsq", "vp0", "msq", "vpe", "lnv", "rstd")})
        for i in range(2):
            E("vector", "reduce_sum", (sl[i]["mean0"][0], bufs[i]), {"axis": AX.X},
              reads=[bkeys[i]], writes=[sl[i]["mean0"][1]])
        for i in range(2):
            dpar = em_dump_next()
            E("scalar", "activation", (dump[:, dpar * 1024:dpar * 1024 + 1024], bufs[i], AF.Square),
              {"accum_out": sl[i]["sumsq"][0]},
              reads=[bkeys[i]], writes=[f"dump:{dpar}", sl[i]["sumsq"][1]])
        for i in range(2):
            E("vector", "tensor_scalar", (sl[i]["mean"][0], sl[i]["mean0"][0], 1.0 / D, None, ALU.mult),
              reads=[sl[i]["mean0"][1]], writes=[sl[i]["mean"][1]])
        for i in range(2):
            E("vector", "tensor_scalar", (sl[i]["vp0"][0], sl[i]["sumsq"][0], 1.0 / D, 1e-5, ALU.mult, ALU.add),
              reads=[sl[i]["sumsq"][1]], writes=[sl[i]["vp0"][1]])
        for i in range(2):
            E("vector", "tensor_tensor", (sl[i]["msq"][0], sl[i]["mean"][0], sl[i]["mean"][0], ALU.mult),
              reads=[sl[i]["mean"][1]], writes=[sl[i]["msq"][1]])
        for i in range(2):
            E("vector", "tensor_tensor", (sl[i]["vpe"][0], sl[i]["vp0"][0], sl[i]["msq"][0], ALU.subtract),
              reads=[sl[i]["vp0"][1], sl[i]["msq"][1]], writes=[sl[i]["vpe"][1]])
        for i in range(2):
            E("scalar", "activation", (sl[i]["lnv"][0], sl[i]["vpe"][0], AF.Ln),
              reads=[sl[i]["vpe"][1]], writes=[sl[i]["lnv"][1]])
        for i in range(2):
            E("scalar", "activation", (sl[i]["rstd"][0], sl[i]["lnv"][0], AF.Exp), {"scale": -0.5},
              reads=[sl[i]["lnv"][1]], writes=[sl[i]["rstd"][1]])

    def emit_ln_finish(sup):
        p = sup % 2
        for i in range(2):
            buf = xld[:, p * 2048 + i * 1024:p * 2048 + (i + 1) * 1024]
            bkey = f"xld{p}{i}"
            mean, meank = slot(f"mean{i}_{p}")
            rstd, rstdk = slot(f"rstd{i}_{p}")
            E("vector", "tensor_scalar",
              (xnbf[:, i * 1024:(i + 1) * 1024], buf, mean, rstd, ALU.subtract, ALU.mult),
              reads=[bkey, meank, rstdk], writes=[f"xnbf:{i}"])
            E("vector", "tensor_copy",
              (xbfres[:, p * 2048 + i * 1024:p * 2048 + (i + 1) * 1024], buf),
              reads=[bkey], writes=[f"xbfres:{p}:{i}"])

    attn_terms = {}

    def emit_rfinal(u):
        NPB = NST // 2
        b, gp = divmod(u, NPB)
        pp = u % 2
        for qp in range(4):
            n = attn_terms[u][qp]
            t_loc = 4 * gp + qp
            cnt = float(SEQ - 128 * (t_loc + 1))
            zs, zsk = slot(f"zs_{qp}")
            zf, zfk = slot(f"zf_{qp}")
            rz, rzk = slot(f"rz_{qp}")
            ckeys = [f"css:{pp}:{qp}:{i}" for i in range(n)]
            E("vector", "reduce_sum", (zs, css[:, pp * 40 + qp * 10:pp * 40 + qp * 10 + n]), {"axis": AX.X},
              reads=ckeys, writes=[zsk])
            E("vector", "tensor_scalar", (zf, zs, cnt, None, ALU.add),
              reads=[zsk], writes=[zfk])
            E("vector", "reciprocal", (rz, zf), reads=[zfk], writes=[rzk])
            E("vector", "tensor_tensor", (rr[:, pp * 4 + qp:pp * 4 + qp + 1], zf, rz, ALU.mult),
              reads=[zfk, rzk], writes=[f"rr:{pp}:{qp}"])

    def emit_attn_pair(u):
        NPB = NST // 2
        b, gp = divmod(u, NPB)
        pp = u % 2
        cbase = pp * 40

        for half in range(2):
            sup = 2 * u + half
            if half == 1:
                emit_ln_finish(sup)
            p = sup % 2
            xntv = xnt[:, p * 2048:(p + 1) * 2048]
            for i in range(2):
                grp = []
                for j in range(NJ):
                    grp.append(("transpose",
                                (psqk_bf[:, i * 1024 + j * 128:i * 1024 + (j + 1) * 128],
                                 xnbf[:, i * 1024 + j * 128:i * 1024 + (j + 1) * 128],
                                 ident[:]), {}))
                E("tensor", None, None, group=grp,
                  reads=[f"xnbf:{i}", "ident"], writes=[f"psqk:{i}"])
                E("vector", "tensor_copy",
                  (xntv.rearrange("p (j c) -> p j c", j=NJ)[:, :, i * 128:(i + 1) * 128],
                   psqk_bf[:, i * 1024:(i + 1) * 1024].rearrange("p (j c) -> p j c", j=NJ)),
                  reads=[f"psqk:{i}"], writes=[f"xnt:{p}"])

        for wsb, wkey, bias_sb, bias_key, dest, destkey in (
                (wq_sb, "wq", bql, "bql", qt, "qt"),
                (wk_sb, "wk", bkl, "bkl", ktcur, "ktcur")):
            for mj in range(NJ):
                bank = mj % 2
                grp = []
                for half in range(2):
                    xntv = xnt[:, (2 * u + half) % 2 * 2048:((2 * u + half) % 2 + 1) * 2048]
                    ps = ps_qk[:, bank * 512 + half * 256:bank * 512 + (half + 1) * 256]
                    for ki in range(NJ):
                        grp.append(("matmul",
                                    (ps, wsb[:, ki * D + mj * 128:ki * D + (mj + 1) * 128],
                                     xntv[:, ki * 256:(ki + 1) * 256]),
                                    {"start": half == 0 and ki == 0,
                                     "stop": half == 1 and ki == NJ - 1}))
                E("tensor", None, None, group=grp,
                  reads=[wkey, "xnt:0", "xnt:1"], writes=[f"psqk:{bank}"])
                E("scalar", "activation",
                  (dest[:, mj * 512:(mj + 1) * 512], ps_qk[:, bank * 512:(bank + 1) * 512],
                   AF.Identity),
                  {"bias": bias_sb[:, mj:mj + 1]},
                  reads=[f"psqk:{bank}", bias_key], writes=[destkey])

        for half in range(2):
            base = (b * NST + 2 * gp + half) * 8 * 128
            DMA("gpsimd", s_ktw,
                kthbm.ap()[base:base + 1024, :].rearrange("(j p) c -> p j c", p=128),
                ktcur[:].rearrange("p (j c) -> p j c", j=NJ)[:, :, half * 256:(half + 1) * 256],
                reads=["ktcur"], writes=[f"kthbm:{b}:{2 * gp + half}"])

        nch = gp

        def load_ktchunk(c):
            cb = c % 2
            for gg in range(2):
                kb = (b * NST + 2 * c + gg) * 8 * 128
                DMA("sync", s_kt[cb],
                    ktbuf[:, cb * 4096:(cb + 1) * 4096].rearrange(
                        "p (j gg c) -> p gg j c", j=NJ, gg=2)[:, gg],
                    kthbm.ap()[kb:kb + 1024, :].rearrange("(j p) c -> p j c", p=128),
                    reads=[f"kthbm:{b}:{2 * c + gg}"],
                    writes=[f"ktbuf:{cb}"])

        for c in range(min(nch, 2)):
            load_ktchunk(c)

        state = [{"n": 0} for _ in range(4)]

        def term(qp, src_ap, src_keys, width):
            st = state[qp]
            idx = st["n"]
            st["n"] += 1
            cs = css[:, cbase + qp * 10 + idx:cbase + qp * 10 + idx + 1]
            csk = f"css:{pp}:{qp}:{idx}"
            dpar = em_dump_next()
            E("scalar", "activation",
              (dump[:, dpar * 1024:dpar * 1024 + width], src_ap, AF.Exp),
              {"scale": 1.0 / 32.0, "accum_out": cs},
              reads=src_keys, writes=[f"dump:{dpar}", csk])

        SB4 = ((ps_s, 0, "pss:0"), (ps_s, 512, "pss:1"),
               (ps_qk, 0, "psqk:0"), (ps_qk, 512, "psqk:1"))

        for c in range(nch):
            cb = c % 2
            for qp in range(4):
                pst, off, bkey_ps = SB4[qp]
                ps = pst[:, off:off + 512]
                grp = []
                for j in range(NJ):
                    grp.append(("matmul",
                                (ps, qt[:, j * 512 + qp * 128:j * 512 + (qp + 1) * 128],
                                 ktbuf[:, cb * 4096 + j * 512:cb * 4096 + (j + 1) * 512]),
                                {"start": j == 0, "stop": j == NJ - 1}))
                E("tensor", None, None, group=grp,
                  reads=["qt", f"ktbuf:{cb}"], writes=[bkey_ps])
                term(qp, ps, [bkey_ps], 512)
            if c + 2 < nch:
                load_ktchunk(c + 2)
        for qp in range(1, 4):
            rem = qp * 128
            start = 0
            while rem > 0:
                w = min(256, rem)
                pst, off, bkey_ps = SB4[qp]
                ps = pst[:, off:off + w]
                grp = []
                for j in range(NJ):
                    grp.append(("matmul",
                                (ps, qt[:, j * 512 + qp * 128:j * 512 + (qp + 1) * 128],
                                 ktcur[:, j * 512 + start:j * 512 + start + w]),
                                {"start": j == 0, "stop": j == NJ - 1}))
                E("tensor", None, None, group=grp, reads=["qt", "ktcur"], writes=[bkey_ps])
                term(qp, ps, [bkey_ps], w)
                start += w
                rem -= w
        for qp in range(4):
            pst, off, bkey_ps = SB4[qp]
            ps = pst[:, off:off + 128]
            grp = []
            for j in range(NJ):
                grp.append(("matmul",
                            (ps, qt[:, j * 512 + qp * 128:j * 512 + (qp + 1) * 128],
                             ktcur[:, j * 512 + qp * 128:j * 512 + (qp + 1) * 128]),
                            {"start": j == 0, "stop": j == NJ - 1}))
            E("tensor", None, None, group=grp, reads=["qt", "ktcur"], writes=[bkey_ps])
            mv = msk[:, 0:128]
            E("vector", "tensor_tensor", (mv, ps, trilm[:], ALU.mult),
              reads=[bkey_ps, "trilm"], writes=["msk"])
            term(qp, mv, ["msk"], 128)
        attn_terms[u] = tuple(state[qp]["n"] for qp in range(4))
        emit_rfinal(u)

    def emit_ffn(sup):
        b, g = divmod(sup, NST)
        p = sup % 2
        pp = (sup // 2) % 2
        xntv = xnt[:, p * 2048:(p + 1) * 2048]
        xntk = f"xnt:{p}"

        def load_w2chunk(k):
            wb = k % 2
            DMA("sync", s_w2[wb],
                w2buf[:, wb * 2048:(wb + 1) * 2048].rearrange("p (ft c) -> p ft c", ft=2),
                w2t_h.ap()[k * 256:(k + 1) * 256, :].rearrange("(ft p) c -> p ft c", p=128),
                reads=[f"w2t:{k // 8}:{j}" for j in range(NJ)], writes=[f"w2buf:{wb}"])

        load_w2chunk(0)
        load_w2chunk(1)
        def emit_h(f):
            bank = f % 2
            ps = ps_qk[:, bank * 512:bank * 512 + 256]
            grp = []
            for ki in range(NJ):
                grp.append(("matmul",
                            (ps, w1t_sb[:, ki * FF + f * 128:ki * FF + (f + 1) * 128],
                             xntv[:, ki * 256:(ki + 1) * 256]),
                            {"start": ki == 0, "stop": ki == NJ - 1}))
            E("tensor", None, None, group=grp, reads=["w1t", xntk], writes=[f"psqk:{bank}"])
            E("scalar", "activation",
              (hT[:, bank * 256:(bank + 1) * 256], ps, AF.Gelu),
              {"bias": b1l[:, f:f + 1]},
              reads=[f"psqk:{bank}", "b1l"], writes=[f"hT:{bank}"])

        def emit_o2(f):
            bank = f % 2
            wb = (f // 2) % 2
            grp = []
            reads = [f"hT:{bank}", f"w2buf:{wb}"]
            if f == 0:
                for s in range(2):
                    for half in range(2):
                        grp.append(("matmul",
                                    (ps_o2[s][:, half * 512:(half + 1) * 512],
                                     onesbf[:],
                                     bvb2[0:1, 1024 + half * 512:1024 + (half + 1) * 512]),
                                    {"start": True, "stop": False}))
                reads += ["onesbf", "bvb2"]
            for s in range(2):
                for half in range(2):
                    grp.append(("matmul",
                                (ps_o2[s][:, half * 512:(half + 1) * 512],
                                 hT[:, bank * 256 + s * 128:bank * 256 + (s + 1) * 128],
                                 w2buf[:, wb * 2048 + (f % 2) * 1024 + half * 512:
                                       wb * 2048 + (f % 2) * 1024 + (half + 1) * 512]),
                                {"start": False, "stop": f == NFT - 1}))
            E("tensor", None, None, group=grp, reads=reads,
              writes=(["pso2a", "pso2b"] if (f == 0 or f == NFT - 1) else []),
              extra=([] if f == 0 else [em.W["pso2a"], em.W["pso2b"]]))

        def emit_v(s):
            grp = []
            for half in range(2):
                ps = ps_s[:, half * 512:(half + 1) * 512]
                grp.append(("matmul",
                            (ps, onesbf[:], bvb2[0:1, half * 512:(half + 1) * 512]),
                            {"start": True, "stop": False}))
                for ki in range(NJ):
                    grp.append(("matmul",
                                (ps, xntv[:, ki * 256 + s * 128:ki * 256 + (s + 1) * 128],
                                 wv_sb[:, ki * D + half * 512:ki * D + (half + 1) * 512]),
                                {"start": False, "stop": ki == NJ - 1}))
            E("tensor", None, None, group=grp,
              reads=["onesbf", "bvb2", xntk, "wv"], writes=["pss:0", "pss:1"])
            qp = (sup % 2) * 2 + s
            E("vector", "scalar_tensor_tensor",
              (outst[:, s * 1024:(s + 1) * 1024], ps_s[:, 0:1024], rr[:, pp * 4 + qp:pp * 4 + qp + 1],
               xbfres[:, p * 2048 + s * 1024:p * 2048 + (s + 1) * 1024],
               ALU.mult, ALU.add),
              reads=["pss:0", "pss:1", f"rr:{pp}:{qp}", f"xbfres:{p}:{s}"],
              writes=[f"outst:{s}"])

        emit_h(0)
        emit_h(1)
        emit_v(0)
        for f in range(2, NFT):
            emit_o2(f - 2)
            if (f - 2) % 2 == 1 and (f - 2) // 2 + 2 < 16:
                load_w2chunk((f - 2) // 2 + 2)
            if f == 4:
                emit_v(1)
            emit_h(f)
        emit_o2(NFT - 2)
        emit_o2(NFT - 1)
        if sup + 2 < NSUP:
            emit_ln_stats(sup + 2)

        for s in range(2):
            E("vector", "scalar_tensor_tensor",
              (outst[:, s * 1024:(s + 1) * 1024], ps_o2[s][:, 0:1024], 0.0,
               outst[:, s * 1024:(s + 1) * 1024], ALU.bypass, ALU.add),
              reads=[("pso2a" if s == 0 else "pso2b"), f"outst:{s}"],
              writes=[f"outst:{s}"])
            r0 = b * SEQ + g * 256 + s * 128
            DMA("gpsimd", s_ow[s], out_h.ap()[r0:r0 + 128, :], outst[:, s * 1024:(s + 1) * 1024],
                reads=[f"outst:{s}"], writes=[])

    NP = NSUP // 2
    emit_x_loads(0)
    emit_ln_stats(0)
    emit_x_loads(1)
    emit_ln_stats(1)
    emit_ln_finish(0)
    emit_attn_pair(0)
    if NSUP > 2:
        emit_x_loads(2)
    emit_prep()
    for u in range(1, NP):
        emit_ffn(2 * u - 2)
        if 2 * u + 1 < NSUP:
            emit_x_loads(2 * u + 1)
        emit_ln_finish(2 * u)
        emit_ffn(2 * u - 1)
        if 2 * u + 2 < NSUP:
            emit_x_loads(2 * u + 2)
        emit_attn_pair(u)
    emit_ffn(NSUP - 2)
    emit_ffn(NSUP - 1)

    em.ops["gpsimd"].append(([(s_ow[0], s_ow[0].n), (s_ow[1], s_ow[1].n), (s_ktw, s_ktw.n)],
                             [], (None, 0)))

    for s in em.sems:
        s.h = ctx.enter_context(nc.semaphore(s.name))
    blk = ctx.enter_context(nc.Block())

    def runner(lst):
        def go(eng):
            for waits, group, (s, amt) in lst:
                for ws, wv in waits:
                    eng.wait_ge(ws.h, wv)
                last = None
                for method, args, kwargs in group:
                    last = getattr(eng, method)(*args, **kwargs)
                if last is not None and s is not None:
                    last.then_inc(s.h, amt)
        return go

    blk.sync(runner(em.ops["sync"]))
    blk.gpsimd(runner(em.ops["gpsimd"]))
    blk.vector(runner(em.ops["vector"]))
    blk.scalar(runner(em.ops["scalar"]))
    blk.tensor(runner(em.ops["tensor"]))
    ctx.close()
    return nc


def _host_inputs(L0_attn_Wq, L0_attn_bq, L0_attn_Wk, L0_attn_bk, L0_attn_Wv, L0_attn_bv,
                 L0_ffn_W1, L0_ffn_b1, L0_ffn_W2, L0_ffn_b2):
    f32 = lambda a: np.ascontiguousarray(np.asarray(a, dtype=np.float32))
    common = {
        "wq": f32(L0_attn_Wq), "wk": f32(L0_attn_Wk), "wv": f32(L0_attn_Wv),
        "w1": f32(L0_ffn_W1), "w2": f32(L0_ffn_W2),
        "bq_l": f32(np.asarray(L0_attn_bq).reshape(8, 128).T),
        "bk_l": f32(np.asarray(L0_attn_bk).reshape(8, 128).T),
        "b1_l": f32(np.asarray(L0_ffn_b1).reshape(32, 128).T),
        "bv_r": f32(np.asarray(L0_attn_bv).reshape(1, D)),
        "b2_r": f32(np.asarray(L0_ffn_b2).reshape(1, D)),
        "ident": np.eye(128, dtype=ml_dtypes.bfloat16),
        "trilm": np.tril(np.ones((128, 128), dtype=ml_dtypes.bfloat16)),
        "onesbf": np.ones((1, 128), dtype=ml_dtypes.bfloat16),
    }
    return common


def kernel(L0_attn_X, L0_attn_Wq, L0_attn_bq, L0_attn_Wk, L0_attn_bk,
           L0_attn_Wv, L0_attn_bv, L0_ffn_W1, L0_ffn_b1, L0_ffn_W2, L0_ffn_b2):
    NB, SEQ, _ = L0_attn_X.shape
    ncores = 8
    bpc = NB // ncores
    X = np.ascontiguousarray(np.asarray(L0_attn_X, dtype=np.float32)).reshape(NB * SEQ, D)
    common = _host_inputs(L0_attn_Wq, L0_attn_bq, L0_attn_Wk, L0_attn_bk, L0_attn_Wv,
                          L0_attn_bv, L0_ffn_W1, L0_ffn_b1, L0_ffn_W2, L0_ffn_b2)
    nc = bass.Bass("TRN2", target_bir_lowering=False)
    build_program(nc, bpc, SEQ)
    in_maps = []
    for c in range(ncores):
        m = dict(common)
        m["x"] = np.ascontiguousarray(X[c * bpc * SEQ:(c + 1) * bpc * SEQ, :])
        in_maps.append(m)
    res = run_bass_kernel_spmd(nc, in_maps, core_ids=list(range(ncores)))
    out = np.concatenate([np.asarray(r["out"]) for r in res.results], axis=0)
    return out.reshape(NB, SEQ, D)
